# version 29
# baseline (speedup 1.0000x reference)
import contextlib
import numpy as np
import ml_dtypes
import concourse.bass as bass
import concourse.mybir as mybir
from concourse.bass_utils import run_bass_kernel_spmd

F32 = mybir.dt.float32
BF16 = mybir.dt.bfloat16
AF = mybir.ActivationFunctionType
ALU = mybir.AluOpType
AXX = mybir.AxisListType.X

N_CORES = 8
BIG_NEG = 1.0e9
LN_EPS = 1.0e-5
NORM_EPS = 1.0e-12


class Sem:
    __slots__ = ("h", "n")

    def __init__(self, h):
        self.h = h
        self.n = 0

    def inc(self, inst, k=1):
        inst.then_inc(self.h, k)
        self.n += k
        return self.n


class Builder:

    def __init__(self, S, D, E, OCW):
        assert D % 128 == 0 and S % 256 == 0 and D % OCW == 0
        self.S, self.D, self.E, self.OCW = S, D, E, OCW
        self.ND = D // 128
        self.NT = S // 128
        self.NTO = self.NT // 2
        self.NQ = D // OCW
        self.RH = D // 4
        self.nc = bass.Bass("TRN2", target_bir_lowering=False)
        self._uid = 0
        self.free = {}
        self.wdma_done = {}
        self.kv_evac = {}
        self.bias_stage_done = {}
        self.psm_ctr = 0
        self._ln2_pend = []

    def _nm(self, base):
        self._uid += 1
        return f"{base}_{self._uid}"

    def _wait_free(self, eng, name):
        fr = self.free.get(name)
        if fr is not None:
            eng.wait_ge(fr[0].h, fr[1])

    def build(self):
        nc = self.nc
        S, D, E, OCW = self.S, self.D, self.E, self.OCW
        ND, NT, NTO, NQ, RH = self.ND, self.NT, self.NTO, self.NQ, self.RH
        S2 = S // 2

        self.x_all = nc.dram_tensor("x_all", [S, D], F32, kind="ExternalInput").ap()
        self.cos_t = nc.dram_tensor("cos_t", [S, RH], BF16, kind="ExternalInput").ap()
        self.sin_t = nc.dram_tensor("sin_t", [S, RH], BF16, kind="ExternalInput").ap()
        self.w_qT = nc.dram_tensor("w_qT", [D, D], F32, kind="ExternalInput").ap()
        self.w_kT = nc.dram_tensor("w_kT", [D, D], F32, kind="ExternalInput").ap()
        self.w_vT = nc.dram_tensor("w_vT", [D, D], F32, kind="ExternalInput").ap()
        self.w_oT = nc.dram_tensor("w_oT", [D, D], F32, kind="ExternalInput").ap()
        self.biases = nc.dram_tensor("biases", [4, D], F32, kind="ExternalInput").ap()
        self.g_wT = nc.dram_tensor("g_wT", [D, E], F32, kind="ExternalInput").ap()
        self.e_w = nc.dram_tensor("e_w", [E, D, D], F32, kind="ExternalInput").ap()
        self.e_b = nc.dram_tensor("e_b", [E, D], F32, kind="ExternalInput").ap()
        self.y = nc.dram_tensor("y", [S2, D], F32, kind="ExternalOutput").ap()
        self.a_hbm = nc.dram_tensor("a_scr", [D], F32).ap()
        self.gt_hbm = nc.dram_tensor("gt_scr", [NTO * E, 128], BF16).ap()

        es0 = contextlib.ExitStack()
        e = es0.enter_context
        self.ident_bf = e(nc.sbuf_tensor("ident_bf", [128, 128], BF16))
        self.iota_f = e(nc.sbuf_tensor("iota_f", [128, 128], F32))
        self.iota_p = e(nc.sbuf_tensor("iota_p", [128, 1], F32))
        self.ones_c = e(nc.sbuf_tensor("ones_c", [128, 1], BF16))
        self.gw_sb = e(nc.sbuf_tensor("gw_sb", [128, ND, E], BF16))
        self.eb_sb = e(nc.sbuf_tensor("eb_sb", [E, D], BF16))
        self.st1 = e(nc.sbuf_tensor("st1", [128, NT, 32], F32))
        self.st2 = e(nc.sbuf_tensor("st2", [128, NTO, 32], F32))
        self.kst = e(nc.sbuf_tensor("kst", [128, NT, 4], F32))
        self.rn_bf = e(nc.sbuf_tensor("rn_bf", [128, 8], BF16))
        self.aT = e(nc.sbuf_tensor("aT", [128, ND], F32))
        self.gates = e(nc.sbuf_tensor("gates", [128, NTO, E], F32))
        self.logit = e(nc.sbuf_tensor("logit", [128, NTO, E], F32))
        self.gst = e(nc.sbuf_tensor("gst", [128, NTO, 8], F32))
        self.eq1s = e(nc.sbuf_tensor("eq1s", [128, NTO, E], F32))
        self.l2s = e(nc.sbuf_tensor("l2s", [128, NTO, E], F32))
        self.gb_scr = e(nc.sbuf_tensor("gb_scr", [128, NTO, E], BF16))
        self.gTb = e(nc.sbuf_tensor("gTb", [8, NTO, 128], BF16))
        self.x_scr = [e(nc.sbuf_tensor(f"x_scr{i}", [128, D], F32)) for i in range(2)]
        self.tmp8 = e(nc.sbuf_tensor("tmp8", [128, 4, RH], BF16))
        self.ct_scr = [e(nc.sbuf_tensor(f"ct{i}", [128, RH], BF16))
                       for i in range(2)]
        self.sn_scr = [e(nc.sbuf_tensor(f"sn{i}", [128, RH], BF16))
                       for i in range(2)]
        self.bb_scr = e(nc.sbuf_tensor("bb_scr", [128, D], BF16, side="right"))
        self.xln_scr = [e(nc.sbuf_tensor(f"xln_scr{i}", [128, D], BF16)) for i in range(2)]
        self.wbuf = [e(nc.sbuf_tensor(f"wbuf{i}", [128, ND, OCW], BF16))
                     for i in range(2)]
        self.sxp = [Sem(e(nc.semaphore(f"sxp{i}"))) for i in range(2)]
        self.sct = [Sem(e(nc.semaphore(f"sct{i}"))) for i in range(2)]
        self.swp = [Sem(e(nc.semaphore(f"swp{i}"))) for i in range(2)]
        self.sbb = Sem(e(nc.semaphore("sbb")))
        self.s0w = Sem(e(nc.semaphore("s0w")))
        self.sa = Sem(e(nc.semaphore("sa")))
        self.sgt = Sem(e(nc.semaphore("sgt")))
        self.sow = Sem(e(nc.semaphore("sow")))
        self.sg = Sem(e(nc.semaphore("sg")))
        self.sc = Sem(e(nc.semaphore("sc")))
        self.sv = Sem(e(nc.semaphore("sv")))
        self.sp = Sem(e(nc.semaphore("sp")))
        self.so = Sem(e(nc.semaphore("so")))

        self.phase0()

        esA = contextlib.ExitStack()
        self.xlnT = esA.enter_context(nc.sbuf_tensor("xlnT", [128, ND, S2], BF16))
        esKV = contextlib.ExitStack()
        self.k_sb = esKV.enter_context(nc.sbuf_tensor("k_sb", [128, NTO, D], BF16))
        self.v_sb = esKV.enter_context(nc.sbuf_tensor("v_sb", [128, NTO, D], BF16))
        with contextlib.ExitStack() as ps_es, nc.Block() as blk:
            self._mk_psum(ps_es)
            ps_a = [ps_es.enter_context(
                nc.psum_tensor(self._nm("ps_a") + str(i), [1, OCW], F32))
                for i in range(NQ)]
            for sp_i in (1, 0):
                self.phase_ln1(blk, sp_i)
                self.phase_proj(blk, sp_i, "k")
                tpq = NTO // NQ

                def after_q(q, sp_i=sp_i, blk=blk, tpq=tpq):
                    if q == NQ - 1:
                        return
                    for t in range(q * tpq, (q + 1) * tpq):
                        self._rope_norm_k_tile(blk, sp_i, t)
                self.phase_proj(blk, sp_i, "v", after_quarter=after_q)
                self.phase_pa(blk, sp_i, ps_a, late=range((NQ - 1) * tpq, NTO))
            self._emit_wdma(blk, "q", 0)
            self.phase_a_roundtrip(blk)
        esKV.close()

        esQ = contextlib.ExitStack()
        esX2 = contextlib.ExitStack()
        self.x2T = esX2.enter_context(
            nc.sbuf_tensor("x2T", [128, ND, S2], BF16, side="right"))
        self.q_sb = esQ.enter_context(
            nc.sbuf_tensor("q_sb", [128, NTO, D], BF16, side="right"))
        esQT = contextlib.ExitStack()
        with contextlib.ExitStack() as ps_es, nc.Block() as blk:
            self._mk_psum(ps_es)
            @blk.gpsimd
            def _(g):
                g.dma_start(
                    out=self.x2T[:, :, 0:OCW],
                    in_=self.w_oT.rearrange("(dc p) o -> p dc o", p=128)
                    [:, :, 0:OCW]).then_inc(self.sow.h, 16)
                self.sow.n += 16
            self.phase_proj(blk, 0, "q")
            esA.close()
            self.qT = esQT.enter_context(
                nc.sbuf_tensor("qT", [128, ND, S2], BF16))
            self.phase_rope_q_transpose(blk)
            for t in range(2):
                xs = self.x_scr[t]
                sxp = self.sxp[t]

                @blk.sync
                def _(s, t=t, xs=xs, sxp=sxp):
                    self._wait_free(s, f"x_scr{t}")
                    if t == 0:
                        s.wait_ge(self.sa.h, self.a_store_done)
                    s.dma_start(out=xs[:, :],
                                in_=self.x_all[t * 128:(t + 1) * 128, :]
                                ).then_inc(sxp.h, 16)
                    sxp.n += 16
                self.x1init_dma = getattr(self, "x1init_dma", {})
                self.x1init_dma[t] = sxp.n
        esQ.close()
        esX1 = contextlib.ExitStack()
        self.x1 = esX1.enter_context(
            nc.sbuf_tensor("x1", [128, NTO, D], F32, side="right"))
        with contextlib.ExitStack() as ps_es, nc.Block() as blk:
            self._mk_psum(ps_es, n_m=4)
            ps_l = ps_es.enter_context(
                nc.psum_tensor(self._nm("ps_l"), [128, E], F32))
            ps_gt = ps_es.enter_context(
                nc.psum_tensor(self._nm("ps_gt"), [self.NTO * 8, 128], BF16))
            self.phase_x1init(blk)
            self._emit_wdma(blk, "o", 1)
            self.phase_oproj(blk, ps_l)
            self.phase_gates(blk, ps_gt)
            self.phase_experts(blk)
            self.phase_biasmix(blk)
        self.phase_store()
        esX1.close()
        esQT.close()
        esX2.close()
        es0.close()
        return self.nc

    def _mk_psum(self, ps_es, tr=True, n_m=2):
        nc, OCW = self.nc, self.OCW
        if tr:
            self.ps_tr = [ps_es.enter_context(
                nc.psum_tensor(self._nm("ps_tr") + str(i), [128, 1024], BF16))
                for i in range(2)]
            for i in range(2):
                self.free.pop(f"tr{i}", None)
        self.ps_m = [ps_es.enter_context(
            nc.psum_tensor(self._nm("ps_m") + str(i), [128, OCW], F32))
            for i in range(n_m)]
        for i in range(4):
            self.free.pop(f"psm{i}", None)

    def phase0(self):
        nc = self.nc
        s0w, sg = self.s0w, self.sg
        with nc.Block() as blk:
            @blk.gpsimd
            def _(g):
                g.dma_start(out=self.gw_sb[:, :, :],
                            in_=self.g_wT.rearrange("(dc p) e -> p dc e", p=128)
                            ).then_inc(s0w.h, 16)
                s0w.n += 16
                g.dma_start(out=self.eb_sb[:, :], in_=self.e_b[:, :]
                            ).then_inc(s0w.h, 16)
                s0w.n += 16
                g.iota(self.iota_f[:, :], [[1, 128]], channel_multiplier=0,
                       allow_small_or_imprecise_dtypes=True)
                g.iota(self.iota_p[:, :1], [[1, 1]], channel_multiplier=1,
                       allow_small_or_imprecise_dtypes=True)
                sg.inc(g.memset(self.ones_c[:, :], 1.0))

            @blk.vector
            def _(v):
                v.wait_ge(sg.h, sg.n)
                v.tensor_scalar(out=self.ident_bf[:, :], in0=self.iota_f[:, :],
                                scalar1=self.iota_p[:, 0:1], scalar2=None,
                                op0=ALU.is_equal)

    def _ln_stats_tile(self, blk, stats, x_in):
        D = self.D
        sbuf, si = stats
        nch = D // 512 if D >= 512 else 1
        cw = D // nch

        @blk.vector
        def _(v):
            v.drain()
            for c in range(nch):
                v.bn_stats(out=sbuf[:, si, 6 * c:6 * c + 6],
                           in_=x_in[:, c * cw:(c + 1) * cw])
            v.drain()
            v.bn_aggr(out=sbuf[:, si, 24:26], in_=sbuf[:, si, 0:6 * nch])

    def _ln_chain(self, blk, sbuf, lo, n):
        sc, sv = self.sc, self.sv
        mean = sbuf[:, lo:lo + n, 24:25]
        var = sbuf[:, lo:lo + n, 25:26]
        vpe = sbuf[:, lo:lo + n, 26:27]
        recip = sbuf[:, lo:lo + n, 27:28]
        rstd = sbuf[:, lo:lo + n, 28:29]
        negmr = sbuf[:, lo:lo + n, 29:30]
        opr = sbuf[:, lo:lo + n, 30:31]

        @blk.vector
        def _(v):
            v.drain()
            v.tensor_scalar(out=vpe, in0=var, scalar1=LN_EPS, scalar2=None,
                            op0=ALU.add)
            v.drain()
            sv.inc(v.reciprocal(out=recip, in_=vpe))
        recip_done = sv.n

        @blk.scalar
        def _(s):
            s.wait_ge(sv.h, recip_done)
            sc.inc(s.activation(out=rstd, in_=recip, func=AF.Sqrt))
        sqrt_done = sc.n

        @blk.vector
        def _(v):
            v.wait_ge(sc.h, sqrt_done)
            v.scalar_tensor_tensor(out=negmr, in0=mean, scalar=-1.0, in1=rstd,
                                   op0=ALU.mult, op1=ALU.mult)
            sv.inc(v.tensor_scalar(out=opr, in0=rstd, scalar1=1.0,
                                   scalar2=None, op0=ALU.add))
        return sv.n

    def _ln_apply(self, blk, stats, x_in, out_bf, ready, xkey):
        sv = self.sv
        sbuf, si = stats

        @blk.vector
        def _(v):
            v.wait_ge(sv.h, ready)
            self._wait_free(v, xkey)
            v.drain()
            sv.inc(v.tensor_scalar(out=out_bf, in0=x_in,
                                   scalar1=sbuf[:, si, 28:29],
                                   scalar2=sbuf[:, si, 29:30],
                                   op0=ALU.mult, op1=ALU.add))
        return sv.n

    def _transpose_tile(self, blk, src_of, dstT, tblock, scalar_evac=True,
                        scale_of=None):
        ND = self.ND
        sp, sv = self.sp, self.sv
        nb = (ND + 7) // 8
        for b in range(nb):
            ps = self.ps_tr[b % 2]
            lo = b * 8
            hi = min(lo + 8, ND)

            @blk.tensor
            def _(t, ps=ps, lo=lo, hi=hi, b=b):
                self._wait_free(t, f"tr{b % 2}")
                for dc in range(lo, hi):
                    tr = t.transpose(ps[:, (dc - lo) * 128:(dc - lo + 1) * 128],
                                     src_of(dc), self.ident_bf[:, :])
                sp.inc(tr)
            tr_done = self.sp.n
            if scale_of is not None:
                @blk.vector
                def _(v, ps=ps, lo=lo, hi=hi, tr_done=tr_done, b=b):
                    v.wait_ge(sp.h, tr_done)
                    if tblock == 0 and b == 0:
                        v.wait_ge(self.sa.h, self.aT_load_done)
                    for dc in range(lo, hi):
                        ins = v.tensor_scalar(
                            out=dstT[:, dc, tblock * 128:(tblock + 1) * 128],
                            in0=ps[:, (dc - lo) * 128:(dc - lo + 1) * 128],
                            scalar1=scale_of(dc), scalar2=None, op0=ALU.mult)
                    sv.inc(ins)
                self.free[f"tr{b % 2}"] = (sv, sv.n)
            elif b % 2 == 0 or not scalar_evac:
                @blk.vector
                def _(v, ps=ps, lo=lo, hi=hi, tr_done=tr_done):
                    v.wait_ge(sp.h, tr_done)
                    sv.inc(v.tensor_copy(
                        out=dstT[:, lo:hi, tblock * 128:(tblock + 1) * 128],
                        in_=ps[:, 0:(hi - lo) * 128]))
                self.free[f"tr{b % 2}"] = (sv, sv.n)
            else:
                @blk.scalar
                def _(s, ps=ps, lo=lo, hi=hi, tr_done=tr_done):
                    s.wait_ge(sp.h, tr_done)
                    self.sc.inc(s.activation(
                        out=dstT[:, lo:hi, tblock * 128:(tblock + 1) * 128],
                        in_=ps[:, 0:(hi - lo) * 128], func=AF.Copy))
                self.free[f"tr{b % 2}"] = (self.sc, self.sc.n)
        self.free["xln_scr_last_tr"] = (sp, self.sp.n)
        self.tile_ready[tblock] = (self.sv.n, self.sc.n)

    def phase_ln1(self, blk, sp_i, after_tile=None):
        NTO = self.NTO
        sc, sv = self.sc, self.sv
        self.tile_ready = {}
        BL = 2
        for t0 in range(0, NTO, BL):
            for t in range(t0, t0 + BL):
                g = sp_i * NTO + t
                xs = self.x_scr[t % 2]
                sxp = self.sxp[t % 2]

                @blk.sync
                def _(s, t=t, g=g, xs=xs, sxp=sxp):
                    self._wait_free(s, f"x_scr{t % 2}")
                    s.dma_start(out=xs[:, :],
                                in_=self.x_all[g * 128:(g + 1) * 128, :]
                                ).then_inc(sxp.h, 16)
                    sxp.n += 16
                dma_done = sxp.n

                @blk.vector
                def _(v, dma_done=dma_done, sxp=sxp):
                    v.wait_ge(sxp.h, dma_done)
                self._ln_stats_tile(blk, (self.st1, g), xs[:, :])
            ready = self._ln_chain(blk, self.st1, sp_i * NTO + t0, BL)
            for t in range(t0, t0 + BL):
                g = sp_i * NTO + t
                xs = self.x_scr[t % 2]
                xln = self.xln_scr[t % 2]
                apply_done = self._ln_apply(blk, (self.st1, g), xs[:, :],
                                            xln[:, :], ready, f"xlnf{t % 2}")
                self.free[f"x_scr{t % 2}"] = (sv, apply_done)

                @blk.tensor
                def _(te, apply_done=apply_done):
                    te.wait_ge(sv.h, apply_done)
                self._transpose_tile(
                    blk, lambda dc, xln=xln: xln[:, dc * 128:(dc + 1) * 128],
                    self.xlnT, t)
                self.free[f"xlnf{t % 2}"] = (self.sp, self.sp.n)
                if after_tile is not None:
                    after_tile(t)

    def _emit_wdma(self, blk, which, q):
        OCW = self.OCW
        wT = {"q": self.w_qT, "k": self.w_kT, "v": self.w_vT,
              "o": self.w_oT}[which]
        swp = self.swp[q % 2]

        @blk.gpsimd
        def _(g, q=q, swp=swp, wT=wT):
            self._wait_free(g, f"wbuf{q % 2}")
            g.dma_start(
                out=self.wbuf[q % 2][:, :, :],
                in_=wT.rearrange("(dc p) o -> p dc o", p=128)
                [:, :, q * OCW:(q + 1) * OCW]).then_inc(swp.h, 16)
            swp.n += 16
        self.wdma_done[(which, q)] = swp.n

    def _get_wdma(self, blk, which, q):
        if (which, q) not in self.wdma_done:
            self._emit_wdma(blk, which, q)
        return self.wdma_done.pop((which, q))

    def _proj_bias(self, blk, which):
        brow = {"q": 0, "k": 1, "v": 2, "o": 3}[which]
        sbb = self.sbb

        @blk.gpsimd
        def _(g, sbb=sbb, brow=brow):
            self._wait_free(g, "bb_scr")
            g.dma_start(out=self.bb_scr[:, :],
                        in_=self.biases[brow:brow + 1, :]
                        .partition_broadcast(128)).then_inc(sbb.h, 16)
            sbb.n += 16
        self.bias_done = sbb.n

    def _proj_qt(self, blk, which, q, t, w_done, first_t):
        NTO, OCW, ND = self.NTO, self.OCW, self.ND
        sc, sv, sp = self.sc, self.sv, self.sp
        dst = {"k": getattr(self, "k_sb", None),
               "v": getattr(self, "v_sb", None),
               "q": getattr(self, "q_sb", None)}[which]
        bias_done = self.bias_done
        self.psm_ctr += 1
        nm_ = len(self.ps_m)
        ps = self.ps_m[self.psm_ctr % nm_]
        pnm = f"psm{self.psm_ctr % nm_}"
        tile_rdy = self.tile_ready.get(t)

        @blk.tensor
        def _(te, q=q, t=t, ps=ps, w_done=w_done, pnm=pnm):
            te.wait_ge(self.swp[q % 2].h, w_done)
            if q == 0 and tile_rdy is not None:
                te.wait_ge(sv.h, tile_rdy[0])
                te.wait_ge(self.sc.h, tile_rdy[1])
            self._wait_free(te, pnm)
            for dc in range(ND):
                mm = te.matmul(
                    ps[:, :],
                    self.xlnT[:, dc, t * 128:(t + 1) * 128],
                    self.wbuf[q % 2][:, dc, :],
                    start=(dc == 0), stop=(dc == ND - 1))
            sp.inc(mm)
        mmc = sp.n

        @blk.vector
        def _(v, q=q, t=t, ps=ps, mmc=mmc, bias_done=bias_done):
            v.wait_ge(sp.h, mmc)
            v.wait_ge(self.sbb.h, bias_done)
            sv.inc(v.tensor_add(
                out=dst[:, t, q * OCW:(q + 1) * OCW],
                in0=ps[:, :],
                in1=self.bb_scr[:, q * OCW:(q + 1) * OCW]))
        self.free[pnm] = (sv, sv.n)
        self.kv_evac[(which, t)] = sv.n

    def phase_proj(self, blk, sp_i, which, after_quarter=None, start_q=0):
        NTO, NQ = self.NTO, self.NQ
        sp = self.sp
        if start_q == 0:
            self._proj_bias(blk, which)
        for q in range(start_q, NQ):
            w_done = self._get_wdma(blk, which, q)
            for t in range(NTO):
                self._proj_qt(blk, which, q, t, w_done, first_t=(t == 0))
            self.free[f"wbuf{q % 2}"] = (sp, sp.n)
            if after_quarter is not None:
                after_quarter(q)
        self.free["bb_scr"] = (self.sv, self.sv.n)

    def _rope_tile(self, blk, buf, t, g, which):
        RH = self.RH
        sv = self.sv
        sct = self.sct[t % 2]
        ct = self.ct_scr[t % 2]
        sn = self.sn_scr[t % 2]

        @blk.sync
        def _(s, t=t, g=g, sct=sct):
            self._wait_free(s, f"ct{t % 2}")
            s.dma_start(out=ct[:, :],
                        in_=self.cos_t[g * 128:(g + 1) * 128, :]
                        ).then_inc(sct.h, 16)
            s.dma_start(out=sn[:, :],
                        in_=self.sin_t[g * 128:(g + 1) * 128, :]
                        ).then_inc(sct.h, 16)
            sct.n += 32
        tab_done = sct.n

        @blk.vector
        def _(v, t=t, tab_done=tab_done, sct=sct):
            v.wait_ge(sct.h, tab_done)
            self._wait_free(v, "tmp8")
            v.drain()
            ev = buf[:, t, 0:RH]
            od = buf[:, t, RH:2 * RH]
            rt = [self.tmp8[:, i, :] for i in range(4)]
            v.tensor_mul(out=rt[0], in0=ev, in1=ct[:, :])
            v.tensor_mul(out=rt[1], in0=od, in1=sn[:, :])
            v.tensor_mul(out=rt[2], in0=od, in1=ct[:, :])
            v.tensor_mul(out=rt[3], in0=ev, in1=sn[:, :])
            v.drain()
            v.tensor_sub(out=ev, in0=rt[0], in1=rt[1])
            sv.inc(v.tensor_add(out=od, in0=rt[2], in1=rt[3]))
        self.free[f"ct{t % 2}"] = (sv, sv.n)
        self.free["tmp8"] = (sv, sv.n)

    def _rope_norm_k_tile(self, blk, sp_i, t):
        sv = self.sv
        g = sp_i * self.NTO + t
        self._rope_tile(blk, self.k_sb, t, g, "k")

        @blk.vector
        def _(v, t=t, g=g):
            self._wait_free(v, "tmp8")
            v.drain()
            sv.inc(v.scalar_tensor_tensor(
                out=self.tmp8.rearrange("p a b -> p (a b)"),
                in0=self.k_sb[:, t, :], scalar=1.0,
                in1=self.k_sb[:, t, :], op0=ALU.mult, op1=ALU.mult,
                accum_out=self.kst[:, g, 0:1]))
        self.free["tmp8"] = (sv, sv.n)
        self.ssq_done = getattr(self, "ssq_done", {})
        self.ssq_done[g] = sv.n

    def _rnorm_batch(self, blk, g0, lo, hi):
        sv = self.sv
        ssq = self.kst[:, g0 + lo:g0 + hi, 0:1]
        nrm = self.kst[:, g0 + lo:g0 + hi, 1:2]
        clm = self.kst[:, g0 + lo:g0 + hi, 2:3]
        rnm = self.kst[:, g0 + lo:g0 + hi, 3:4]
        ssq_all = self.ssq_done[g0 + hi - 1]

        @blk.scalar
        def _(s, ssq_all=ssq_all, ssq=ssq, nrm=nrm):
            s.wait_ge(sv.h, ssq_all)
            self.sc.inc(s.activation(out=nrm, in_=ssq, func=AF.Sqrt))
        nrm_done = self.sc.n

        @blk.vector
        def _(v, nrm_done=nrm_done, nrm=nrm, clm=clm, rnm=rnm, lo=lo, hi=hi):
            v.wait_ge(self.sc.h, nrm_done)
            v.tensor_scalar(out=clm, in0=nrm, scalar1=NORM_EPS, scalar2=None,
                            op0=ALU.max)
            v.drain()
            v.reciprocal(out=rnm, in_=clm)
            v.drain()
            sv.inc(v.tensor_copy(out=self.rn_bf[:, lo:hi], in_=rnm))
        return sv.n

    def phase_pa(self, blk, sp_i, ps_a, late=()):
        NTO, NQ, OCW = self.NTO, self.NQ, self.OCW
        sv, sp = self.sv, self.sp
        first = sp_i == 1
        last = sp_i == 0
        late = set(late)
        order = [t for t in range(NTO) if t not in late] + sorted(late)
        g0 = sp_i * NTO
        n_early = NTO - len(late)
        rn_early = self._rnorm_batch(blk, g0, 0, n_early) if n_early else None
        rn_of = lambda t: rn_early if t < n_early else rn_late

        def pa_tile(t):
            is_late = t in late
            @blk.vector
            def _(v, t=t, is_late=is_late):
                if t == order[0] or is_late:
                    v.drain()
                sv.inc(v.tensor_mul(out=self.v_sb[:, t, :],
                                    in0=self.k_sb[:, t, :],
                                    in1=self.v_sb[:, t, :]))
            p_done = sv.n

            is_first = first and t == order[0]
            is_last = last and t == order[-1]

            @blk.tensor
            def _(te, t=t, p_done=p_done, is_first=is_first, is_last=is_last,
                  rn_done=rn_of(t)):
                te.wait_ge(sv.h, p_done)
                te.wait_ge(sv.h, rn_done)
                for qi in range(NQ):
                    mm = te.matmul(ps_a[qi][:, :], self.rn_bf[:, t:t + 1],
                                   self.v_sb[:, t, qi * OCW:(qi + 1) * OCW],
                                   start=is_first, stop=is_last)
                if is_last:
                    sp.inc(mm)

        for t in order[:n_early]:
            pa_tile(t)
        for t in sorted(late):
            self._rope_norm_k_tile(blk, sp_i, t)
        rn_late = self._rnorm_batch(blk, g0, n_early, NTO) if late else None
        for t in order[n_early:]:
            pa_tile(t)
        if last:
            a_done = self.sp.n

            @blk.vector
            def _(v):
                v.wait_ge(sp.h, a_done)
                self._wait_free(v, "x_scr0")
                v.drain()
                for qi in range(NQ):
                    ins = v.tensor_copy(
                        out=self.x_scr[0][0:1, qi * OCW:(qi + 1) * OCW],
                        in_=ps_a[qi][:, :])
                self.sv.inc(ins)
            self.free["x_scr0"] = (self.sv, self.sv.n)
            self.a_row_ready = self.sv.n

    def phase_a_roundtrip(self, blk):
        sa, sv = self.sa, self.sv
        a_done = self.a_row_ready

        @blk.sync
        def _(s):
            s.wait_ge(sv.h, a_done)
            s.dma_start(out=self.a_hbm[None, :], in_=self.x_scr[0][0:1, :]
                        ).then_inc(sa.h, 16)
            sa.n += 16
            s.wait_ge(sa.h, sa.n)
            self.a_store_done = sa.n
            s.dma_start(out=self.aT[:, :],
                        in_=self.a_hbm.rearrange("(dc p) -> p dc", p=128),
                        allow_slow_non_contiguous=True,
                        ).then_inc(sa.h, 16)
            sa.n += 16
        self.aT_load_done = sa.n

    def phase_rope_q_transpose(self, blk):
        ND, NTO = self.ND, self.NTO
        sa, sv = self.sa, self.sv
        self.tile_ready = {}
        for t in range(NTO):
            self._rope_tile(blk, self.q_sb, t, t, "q")
            rope_done = sv.n

            @blk.tensor
            def _(te, rope_done=rope_done):
                te.wait_ge(sv.h, rope_done)
            self._transpose_tile(
                blk, lambda dc, t=t: self.q_sb[:, t, dc * 128:(dc + 1) * 128],
                self.qT, t,
                scale_of=lambda dc: self.aT[:, dc:dc + 1])

    def _scale_wquarter(self, blk, which, q):
        ND = self.ND
        sv, sa = self.sv, self.sa
        swp = self.swp[q % 2]
        w_done = self._get_wdma(blk, which, q)

        @blk.vector
        def _(v, q=q, w_done=w_done, swp=swp):
            v.wait_ge(swp.h, w_done)
            v.wait_ge(sa.h, self.aT_load_done)
            for dc in range(ND):
                ins = v.tensor_scalar(out=self.wbuf[q % 2][:, dc, :],
                                      in0=self.wbuf[q % 2][:, dc, :],
                                      scalar1=self.aT[:, dc:dc + 1],
                                      scalar2=None, op0=ALU.mult)
            sv.inc(ins)
        self.wscale_done = getattr(self, "wscale_done", {})
        self.wscale_done[q] = (sv.n, w_done)

    def phase_x1init(self, blk):
        NTO = self.NTO
        sv = self.sv
        self._proj_bias(blk, "o")
        for t in range(NTO):
            xs = self.x_scr[t % 2]
            sxp = self.sxp[t % 2]
            if t < 2:
                dma_done = self.x1init_dma[t]
            else:
                @blk.sync
                def _(s, t=t, xs=xs, sxp=sxp):
                    self._wait_free(s, f"x_scr{t % 2}")
                    s.dma_start(out=xs[:, :],
                                in_=self.x_all[t * 128:(t + 1) * 128, :]
                                ).then_inc(sxp.h, 16)
                    sxp.n += 16
                dma_done = sxp.n

            @blk.vector
            def _(v, t=t, xs=xs, dma_done=dma_done, sxp=sxp):
                v.wait_ge(sxp.h, dma_done)
                if t == 0:
                    v.wait_ge(self.sbb.h, self.bias_done)
                v.drain()
                v.tensor_scalar(out=self.x1[:, t, :], in0=xs[:, :],
                                scalar1=self.st1[:, t, 30:31],
                                scalar2=self.st1[:, t, 29:30],
                                op0=ALU.mult, op1=ALU.add)
                v.drain()
                sv.inc(v.tensor_add(out=self.x1[:, t, :],
                                    in0=self.x1[:, t, :],
                                    in1=self.bb_scr[:, :]))
            self.free[f"x_scr{t % 2}"] = (sv, sv.n)

    def phase_oproj(self, blk, ps_l):
        NTO, NQ, OCW, ND = self.NTO, self.NQ, self.OCW, self.ND
        sv, sp = self.sv, self.sp
        self.tile_ready = {}
        for q in range(NQ):
            if q == 0:
                swp, w_done = self.sow, 16
                rhs_of = lambda dc: self.x2T[:, dc, 0:OCW]
            else:
                swp = self.swp[q % 2]
                w_done = self._get_wdma(blk, "o", q)
                rhs_of = lambda dc, q=q: self.wbuf[q % 2][:, dc, :]
            for t in range(NTO):
                nm_ = len(self.ps_m)
                ps = self.ps_m[(q * NTO + t) % nm_]
                pnm = f"psm{(q * NTO + t) % nm_}"

                @blk.tensor
                def _(te, q=q, t=t, ps=ps, w_done=w_done, swp=swp, pnm=pnm,
                      rhs_of=rhs_of):
                    te.wait_ge(swp.h, w_done)
                    self._wait_free(te, pnm)
                    for dc in range(ND):
                        mm = te.matmul(
                            ps[:, :],
                            self.qT[:, dc, t * 128:(t + 1) * 128],
                            rhs_of(dc),
                            start=(dc == 0), stop=(dc == ND - 1))
                    sp.inc(mm)
                mmc = sp.n

                @blk.vector
                def _(v, q=q, t=t, ps=ps, mmc=mmc):
                    v.wait_ge(sp.h, mmc)
                    if q == 0:
                        v.drain()
                    sv.inc(v.scalar_tensor_tensor(
                        out=self.x1[:, t, q * OCW:(q + 1) * OCW],
                        in0=ps[:, :], scalar=1.0,
                        in1=self.x1[:, t, q * OCW:(q + 1) * OCW],
                        op0=ALU.mult, op1=ALU.add))
                self.free[pnm] = (sv, sv.n)
                if q == NQ - 1:
                    self._ln2_tile(blk, ps_l, t)
            if q != 0:
                self.free[f"wbuf{q % 2}"] = (sp, sp.n)
        self.free["bb_scr"] = (sv, self.sv.n)

    def _ln2_tile(self, blk, ps_l, t):
        self._ln_stats_tile(blk, (self.st2, t), self.x1[:, t, :])
        self._ln2_pend.append(t)
        if len(self._ln2_pend) == 4 or t == self.NTO - 1:
            tiles = self._ln2_pend
            self._ln2_pend = []
            ready = self._ln_chain(blk, self.st2, tiles[0], len(tiles))
            for tt in tiles:
                self._ln2_tail(blk, ps_l, tt, ready)

    def _ln2_tail(self, blk, ps_l, t, ready):
        ND = self.ND
        sc, sv, sp, s0w = self.sc, self.sv, self.sp, self.s0w
        xln = self.xln_scr[t % 2]
        apply_done = self._ln_apply(blk, (self.st2, t), self.x1[:, t, :],
                                    xln[:, :], ready, f"xlnf{t % 2}")

        @blk.tensor
        def _(te, apply_done=apply_done):
            te.wait_ge(sv.h, apply_done)
        self._transpose_tile(
            blk, lambda dc, xln=xln: xln[:, dc * 128:(dc + 1) * 128],
            self.x2T, t, scalar_evac=False)
        self.free[f"xlnf{t % 2}"] = (self.sp, self.sp.n)
        x2T_done = self.tile_ready[t]

        @blk.tensor
        def _(te, t=t, x2T_done=x2T_done):
            te.wait_ge(s0w.h, 32)
            self._wait_free(te, "ps_l")
            te.wait_ge(sv.h, x2T_done[0])
            te.wait_ge(sc.h, x2T_done[1])
            for dc in range(ND):
                mm = te.matmul(ps_l[:, :],
                               self.x2T[:, dc, t * 128:(t + 1) * 128],
                               self.gw_sb[:, dc, :],
                               start=(dc == 0), stop=(dc == ND - 1))
            self.sp.inc(mm)
        lg_done = self.sp.n

        @blk.vector
        def _(v, t=t, lg_done=lg_done):
            v.wait_ge(sp.h, lg_done)
            sv.inc(v.tensor_copy(out=self.logit[:, t, :], in_=ps_l[:, :]))
        self.free["ps_l"] = (sv, sv.n)

    def phase_gates(self, blk, ps_gt):
        NTO, E = self.NTO, self.E
        sc, sv, sp = self.sc, self.sv, self.sp
        L = self.logit[:, :, :]
        m1 = self.gst[:, :, 0:1]
        m2 = self.gst[:, :, 1:2]
        w2 = self.gst[:, :, 2:3]
        w1 = self.gst[:, :, 3:4]

        def bc(ap):
            return ap.broadcast_to([128, NTO, E])

        @blk.vector
        def _(v):
            v.drain()
            v.tensor_reduce(out=m1, in_=L, axis=AXX, op=ALU.max)
            v.drain()
            v.tensor_tensor(out=self.eq1s[:, :, :], in0=L, in1=bc(m1),
                            op=ALU.is_equal)
            v.drain()
            v.scalar_tensor_tensor(out=self.l2s[:, :, :],
                                   in0=self.eq1s[:, :, :],
                                   scalar=-BIG_NEG, in1=L,
                                   op0=ALU.mult, op1=ALU.add)
            v.drain()
            v.tensor_reduce(out=m2, in_=self.l2s[:, :, :], axis=AXX,
                            op=ALU.max)
            v.drain()
            sv.inc(v.tensor_tensor(out=w2, in0=m2, in1=m1, op=ALU.subtract))
        d_done = sv.n

        @blk.scalar
        def _(s, d_done=d_done):
            s.wait_ge(sv.h, d_done)
            sc.inc(s.activation(out=w2, in_=w2, func=AF.Sigmoid))
        sig_done = sc.n

        @blk.vector
        def _(v, sig_done=sig_done):
            v.wait_ge(sc.h, sig_done)
            v.tensor_scalar(out=w1, in0=w2, scalar1=-1.0, scalar2=1.0,
                            op0=ALU.mult, op1=ALU.add)
            v.drain()
            v.tensor_tensor(out=self.l2s[:, :, :], in0=self.l2s[:, :, :],
                            in1=bc(m2), op=ALU.is_equal)
            v.drain()
            v.tensor_tensor(out=self.l2s[:, :, :], in0=self.l2s[:, :, :],
                            in1=bc(w2), op=ALU.mult)
            v.tensor_tensor(out=self.eq1s[:, :, :], in0=self.eq1s[:, :, :],
                            in1=bc(w1), op=ALU.mult)
            v.drain()
            v.tensor_tensor(out=self.gates[:, :, :], in0=self.eq1s[:, :, :],
                            in1=self.l2s[:, :, :], op=ALU.add)
            v.drain()
            sv.inc(v.tensor_copy(out=self.gb_scr[:, :, :],
                                 in_=self.gates[:, :, :]))
        g_done = sv.n

        @blk.tensor
        def _(te, g_done=g_done):
            te.wait_ge(sv.h, g_done)
            self.sp.inc(te.transpose(
                ps_gt[:, :],
                self.gb_scr.rearrange("p t e -> p (t e)"),
                self.ident_bf[:, :]))
        gt_done = self.sp.n

        gpk = self.tmp8.rearrange("p a b -> p (a b)")[0:NTO * E, 0:128]

        @blk.vector
        def _(v, gt_done=gt_done):
            v.wait_ge(sp.h, gt_done)
            self._wait_free(v, "tmp8")
            sv.inc(v.tensor_copy(out=gpk, in_=ps_gt[:, :]))
        self.free["ps_gt_ev"] = (sv, sv.n)
        pk_done = sv.n

        @blk.sync
        def _(s, pk_done=pk_done, gpk=gpk):
            s.wait_ge(sv.h, pk_done)
            s.dma_start(out=self.gt_hbm[:, :], in_=gpk
                        ).then_inc(self.sgt.h, 16)
            self.sgt.n += 16
            s.wait_ge(self.sgt.h, self.sgt.n)
            s.dma_start(out=self.gTb[:, :, :],
                        in_=self.gt_hbm.rearrange("(t e) n -> e t n", e=8)
                        ).then_inc(self.sgt.h, 16)
            self.sgt.n += 16
        self.free["ps_gt"] = (self.sgt, self.sgt.n)

    def phase_biasmix(self, blk):
        NTO, NQ, OCW = self.NTO, self.NQ, self.OCW
        sv, sp, s0w = self.sv, self.sp, self.s0w
        so = self.so
        gTb_sem, gTb_done = self.free["ps_gt"]
        for t in range(NTO):
            for q in range(NQ):
                nm_ = len(self.ps_m)
                ps = self.ps_m[(t * NQ + q) % nm_]
                pnm = f"psm{(t * NQ + q) % nm_}"

                @blk.tensor
                def _(te, t=t, q=q, ps=ps, pnm=pnm):
                    te.wait_ge(s0w.h, 32)
                    if (t, q) == (0, 0):
                        te.wait_ge(gTb_sem.h, gTb_done)
                    self._wait_free(te, pnm)
                    self.sp.inc(te.matmul(
                        ps[:, :], self.gTb[:, t, :],
                        self.eb_sb[:, q * OCW:(q + 1) * OCW],
                        start=True, stop=True))
                mm_done = self.sp.n

                @blk.vector
                def _(v, t=t, q=q, ps=ps, mm_done=mm_done):
                    v.wait_ge(sp.h, mm_done)
                    if (t, q) == (0, 0):
                        v.drain()
                    sv.inc(v.scalar_tensor_tensor(
                        out=self.x1[:, t, q * OCW:(q + 1) * OCW],
                        in0=ps[:, :], scalar=1.0,
                        in1=self.x1[:, t, q * OCW:(q + 1) * OCW],
                        op0=ALU.mult, op1=ALU.add))
                self.free[pnm] = (sv, sv.n)
            tile_done = sv.n

            @blk.sync
            def _(s, t=t, tile_done=tile_done):
                s.wait_ge(sv.h, tile_done)
                s.dma_start(out=self.y[t * 128:(t + 1) * 128, :],
                            in_=self.x1[:, t, :]).then_inc(so.h, 16)
                so.n += 16

    def phase_experts(self, blk):
        ND, NTO, NQ, OCW, E = self.ND, self.NTO, self.NQ, self.OCW, self.E
        sv, sp = self.sv, self.sp
        steps = [(e_i, q) for e_i in range(E) for q in range(NQ)]
        for si, (e_i, q) in enumerate(steps):
            swp = self.swp[si % 2]

            @blk.gpsimd
            def _(g, si=si, e_i=e_i, q=q, swp=swp):
                self._wait_free(g, f"wbuf{si % 2}")
                g.dma_start(
                    out=self.wbuf[si % 2][:, :, :],
                    in_=self.e_w.rearrange("e (dc p) o -> e p dc o", p=128)
                    [e_i, :, :, q * OCW:(q + 1) * OCW]).then_inc(swp.h, 16)
                swp.n += 16
            w_done = swp.n
            for t in range(NTO):
                nm_ = len(self.ps_m)
                ps = self.ps_m[(si * NTO + t) % nm_]
                pnm = f"psm{(si * NTO + t) % nm_}"

                x2rdy = self.tile_ready[t]

                @blk.tensor
                def _(te, si=si, t=t, ps=ps, w_done=w_done, swp=swp, pnm=pnm,
                      x2rdy=x2rdy):
                    te.wait_ge(swp.h, w_done)
                    if si == 0:
                        te.wait_ge(sv.h, x2rdy[0])
                        te.wait_ge(self.sc.h, x2rdy[1])
                    self._wait_free(te, pnm)
                    for dc in range(ND):
                        mm = te.matmul(
                            ps[:, :],
                            self.x2T[:, dc, t * 128:(t + 1) * 128],
                            self.wbuf[si % 2][:, dc, :],
                            start=(dc == 0), stop=(dc == ND - 1))
                    self.sp.inc(mm)
                mmc = self.sp.n

                @blk.vector
                def _(v, e_i=e_i, q=q, t=t, ps=ps, mmc=mmc):
                    v.wait_ge(sp.h, mmc)
                    if t == 0:
                        v.drain()
                    sv.inc(v.scalar_tensor_tensor(
                        out=self.x1[:, t, q * OCW:(q + 1) * OCW],
                        in0=ps[:, :], scalar=self.gates[:, t, e_i:e_i + 1],
                        in1=self.x1[:, t, q * OCW:(q + 1) * OCW],
                        op0=ALU.mult, op1=ALU.add))
                self.free[pnm] = (sv, sv.n)
            self.free[f"wbuf{si % 2}"] = (sp, self.sp.n)

    def phase_store(self):
        so = self.so
        with self.nc.Block() as blk:
            @blk.sync
            def _(s):
                s.wait_ge(so.h, so.n)


def _host_prep(L0_attn_W_in, L0_attn_b_in, L0_attn_W_out, L0_attn_b_out,
               L0_ffn_gate_w, L0_ffn_expert_w, L0_ffn_expert_b, S, D):
    rot = D // 2
    perm = np.concatenate([np.arange(0, rot, 2), np.arange(1, rot, 2),
                           np.arange(rot, D)])
    w_q = L0_attn_W_in[0:D][perm]
    w_k = L0_attn_W_in[D:2 * D][perm]
    w_v = L0_attn_W_in[2 * D:3 * D][perm]
    w_qT = np.ascontiguousarray(w_q.T)
    w_kT = np.ascontiguousarray(w_k.T)
    w_vT = np.ascontiguousarray(w_v.T)
    w_oT = np.ascontiguousarray(L0_attn_W_out[:, perm].T)
    biases = np.ascontiguousarray(np.stack([
        L0_attn_b_in[0:D][perm], L0_attn_b_in[D:2 * D][perm],
        L0_attn_b_in[2 * D:3 * D][perm], L0_attn_b_out]).astype(np.float32))
    g_wT = np.ascontiguousarray(L0_ffn_gate_w.T)
    e_w = np.ascontiguousarray(L0_ffn_expert_w)
    e_b = np.ascontiguousarray(L0_ffn_expert_b)
    inv = 1.0 / (10000.0 ** (np.arange(0, rot, 2, dtype=np.float64) / rot))
    ang = np.arange(S, dtype=np.float64)[:, None] * inv[None, :]
    cos_t = np.cos(ang).astype(ml_dtypes.bfloat16)
    sin_t = np.sin(ang).astype(ml_dtypes.bfloat16)
    return dict(w_qT=w_qT, w_kT=w_kT, w_vT=w_vT, w_oT=w_oT, biases=biases,
                g_wT=g_wT, e_w=e_w, e_b=e_b, cos_t=cos_t, sin_t=sin_t)


def _in_maps(X, shared, S, n_cores):
    S2 = S // 2
    maps = []
    for c in range(n_cores):
        b, h = c // 2, c % 2
        xb = np.ascontiguousarray(X[b])
        if h == 0:
            x_all, cos_c, sin_c = xb, shared["cos_t"], shared["sin_t"]
        else:
            x_all = np.ascontiguousarray(
                np.concatenate([xb[S2:], xb[:S2]], axis=0))
            cos_c = np.ascontiguousarray(np.concatenate(
                [shared["cos_t"][S2:], shared["cos_t"][:S2]], axis=0))
            sin_c = np.ascontiguousarray(np.concatenate(
                [shared["sin_t"][S2:], shared["sin_t"][:S2]], axis=0))
        m = dict(shared)
        m["x_all"] = x_all
        m["cos_t"] = cos_c
        m["sin_t"] = sin_c
        maps.append(m)
    return maps


def kernel(L0_attn_X, L0_attn_W_in, L0_attn_b_in, L0_attn_W_out, L0_attn_b_out,
           L0_ffn_gate_w, L0_ffn_expert_w, L0_ffn_expert_b):
    B, S, D = L0_attn_X.shape
    E = L0_ffn_gate_w.shape[0]
    shared = _host_prep(np.asarray(L0_attn_W_in, dtype=np.float32),
                        np.asarray(L0_attn_b_in, dtype=np.float32),
                        np.asarray(L0_attn_W_out, dtype=np.float32),
                        np.asarray(L0_attn_b_out, dtype=np.float32),
                        np.asarray(L0_ffn_gate_w, dtype=np.float32),
                        np.asarray(L0_ffn_expert_w, dtype=np.float32),
                        np.asarray(L0_ffn_expert_b, dtype=np.float32), S, D)
    X = np.asarray(L0_attn_X, dtype=np.float32)
    maps = _in_maps(X, shared, S, N_CORES)
    nc = Builder(S, D, E, 512).build()
    res = run_bass_kernel_spmd(nc, maps, core_ids=list(range(N_CORES)))
    S2 = S // 2
    out = np.empty((B, S, D), dtype=np.float32)
    for c in range(N_CORES):
        b, h = c // 2, c % 2
        out[b, h * S2:(h + 1) * S2] = res.results[c]["y"]
    return out


# version 30
# speedup vs baseline: 1.0076x; 1.0076x over previous
import contextlib
import numpy as np
import ml_dtypes
import concourse.bass as bass
import concourse.mybir as mybir
from concourse.bass_utils import run_bass_kernel_spmd

F32 = mybir.dt.float32
BF16 = mybir.dt.bfloat16
AF = mybir.ActivationFunctionType
ALU = mybir.AluOpType
AXX = mybir.AxisListType.X

N_CORES = 8
BIG_NEG = 1.0e9
LN_EPS = 1.0e-5
NORM_EPS = 1.0e-12


class Sem:
    __slots__ = ("h", "n")

    def __init__(self, h):
        self.h = h
        self.n = 0

    def inc(self, inst, k=1):
        inst.then_inc(self.h, k)
        self.n += k
        return self.n


class Builder:

    def __init__(self, S, D, E, OCW):
        assert D % 128 == 0 and S % 256 == 0 and D % OCW == 0
        self.S, self.D, self.E, self.OCW = S, D, E, OCW
        self.ND = D // 128
        self.NT = S // 128
        self.NTO = self.NT // 2
        self.NQ = D // OCW
        self.RH = D // 4
        self.nc = bass.Bass("TRN2", target_bir_lowering=False)
        self._uid = 0
        self.free = {}
        self.wdma_done = {}
        self.kv_evac = {}
        self.bias_stage_done = {}
        self.psm_ctr = 0
        self._ln2_pend = []

    def _nm(self, base):
        self._uid += 1
        return f"{base}_{self._uid}"

    def _wait_free(self, eng, name):
        fr = self.free.get(name)
        if fr is not None:
            eng.wait_ge(fr[0].h, fr[1])

    def build(self):
        nc = self.nc
        S, D, E, OCW = self.S, self.D, self.E, self.OCW
        ND, NT, NTO, NQ, RH = self.ND, self.NT, self.NTO, self.NQ, self.RH
        S2 = S // 2

        self.x_all = nc.dram_tensor("x_all", [S, D], F32, kind="ExternalInput").ap()
        self.cos_t = nc.dram_tensor("cos_t", [S, RH], BF16, kind="ExternalInput").ap()
        self.sin_t = nc.dram_tensor("sin_t", [S, RH], BF16, kind="ExternalInput").ap()
        self.w_qT = nc.dram_tensor("w_qT", [D, D], F32, kind="ExternalInput").ap()
        self.w_kT = nc.dram_tensor("w_kT", [D, D], F32, kind="ExternalInput").ap()
        self.w_vT = nc.dram_tensor("w_vT", [D, D], F32, kind="ExternalInput").ap()
        self.w_oT = nc.dram_tensor("w_oT", [D, D], F32, kind="ExternalInput").ap()
        self.biases = nc.dram_tensor("biases", [4, D], F32, kind="ExternalInput").ap()
        self.g_wT = nc.dram_tensor("g_wT", [D, E], F32, kind="ExternalInput").ap()
        self.e_w = nc.dram_tensor("e_w", [E, D, D], F32, kind="ExternalInput").ap()
        self.e_b = nc.dram_tensor("e_b", [E, D], F32, kind="ExternalInput").ap()
        self.y = nc.dram_tensor("y", [S2, D], F32, kind="ExternalOutput").ap()
        self.a_hbm = nc.dram_tensor("a_scr", [D], F32).ap()
        self.gt_hbm = nc.dram_tensor("gt_scr", [NTO * E, 128], BF16).ap()

        es0 = contextlib.ExitStack()
        e = es0.enter_context
        self.ident_bf = e(nc.sbuf_tensor("ident_bf", [128, 128], BF16))
        self.iota_f = e(nc.sbuf_tensor("iota_f", [128, 128], F32))
        self.iota_p = e(nc.sbuf_tensor("iota_p", [128, 1], F32))
        self.ones_c = e(nc.sbuf_tensor("ones_c", [128, 1], BF16))
        self.gw_sb = e(nc.sbuf_tensor("gw_sb", [128, ND, E], BF16))
        self.eb_sb = e(nc.sbuf_tensor("eb_sb", [E, D], BF16))
        self.st1 = e(nc.sbuf_tensor("st1", [128, NT, 32], F32))
        self.st2 = e(nc.sbuf_tensor("st2", [128, NTO, 32], F32))
        self.kst = e(nc.sbuf_tensor("kst", [128, NT, 4], F32))
        self.rn_bf = e(nc.sbuf_tensor("rn_bf", [128, 8], BF16))
        self.aT = e(nc.sbuf_tensor("aT", [128, ND], F32))
        self.gates = e(nc.sbuf_tensor("gates", [128, NTO, E], F32))
        self.logit = e(nc.sbuf_tensor("logit", [128, NTO, E], F32))
        self.gst = e(nc.sbuf_tensor("gst", [128, NTO, 8], F32))
        self.eq1s = e(nc.sbuf_tensor("eq1s", [128, NTO, E], F32))
        self.l2s = e(nc.sbuf_tensor("l2s", [128, NTO, E], F32))
        self.gb_scr = e(nc.sbuf_tensor("gb_scr", [128, NTO, E], BF16))
        self.gTb = e(nc.sbuf_tensor("gTb", [8, NTO, 128], BF16))
        self.x_scr = [e(nc.sbuf_tensor(f"x_scr{i}", [128, D], F32)) for i in range(2)]
        self.tmp8 = e(nc.sbuf_tensor("tmp8", [128, 4, RH], BF16))
        self.ct_scr = [e(nc.sbuf_tensor(f"ct{i}", [128, RH], BF16))
                       for i in range(2)]
        self.sn_scr = [e(nc.sbuf_tensor(f"sn{i}", [128, RH], BF16))
                       for i in range(2)]
        self.bb_scr = e(nc.sbuf_tensor("bb_scr", [128, D], BF16, side="right"))
        self.xln_scr = [e(nc.sbuf_tensor(f"xln_scr{i}", [128, D], BF16)) for i in range(2)]
        self.wbuf = [e(nc.sbuf_tensor(f"wbuf{i}", [128, ND, OCW], BF16))
                     for i in range(2)]
        self.sxp = [Sem(e(nc.semaphore(f"sxp{i}"))) for i in range(2)]
        self.sct = [Sem(e(nc.semaphore(f"sct{i}"))) for i in range(2)]
        self.swp = [Sem(e(nc.semaphore(f"swp{i}"))) for i in range(2)]
        self.sbb = Sem(e(nc.semaphore("sbb")))
        self.s0w = Sem(e(nc.semaphore("s0w")))
        self.sa = Sem(e(nc.semaphore("sa")))
        self.sgt = Sem(e(nc.semaphore("sgt")))
        self.sow = Sem(e(nc.semaphore("sow")))
        self.sg = Sem(e(nc.semaphore("sg")))
        self.sc = Sem(e(nc.semaphore("sc")))
        self.sv = Sem(e(nc.semaphore("sv")))
        self.sp = Sem(e(nc.semaphore("sp")))
        self.so = Sem(e(nc.semaphore("so")))

        self.phase0()

        esA = contextlib.ExitStack()
        self.xlnT = esA.enter_context(nc.sbuf_tensor("xlnT", [128, ND, S2], BF16))
        esKV = contextlib.ExitStack()
        self.k_sb = esKV.enter_context(nc.sbuf_tensor("k_sb", [128, NTO, D], BF16))
        self.v_sb = esKV.enter_context(nc.sbuf_tensor("v_sb", [128, NTO, D], BF16))
        with contextlib.ExitStack() as ps_es, nc.Block() as blk:
            self._mk_psum(ps_es)
            ps_a = [ps_es.enter_context(
                nc.psum_tensor(self._nm("ps_a") + str(i), [1, OCW], F32))
                for i in range(NQ)]
            for sp_i in (1, 0):
                self.phase_ln1(blk, sp_i)
                self.phase_proj(blk, sp_i, "k")
                tpq = NTO // NQ

                def after_q(q, sp_i=sp_i, blk=blk, tpq=tpq):
                    if q == NQ - 1:
                        return
                    for t in range(q * tpq, (q + 1) * tpq):
                        self._rope_norm_k_tile(blk, sp_i, t)
                self.phase_proj(blk, sp_i, "v", after_quarter=after_q)
                self.phase_pa(blk, sp_i, ps_a, late=range((NQ - 1) * tpq, NTO))
            self._emit_wdma(blk, "q", 0)
            self.phase_a_roundtrip(blk)
        esKV.close()

        esQ = contextlib.ExitStack()
        esX2 = contextlib.ExitStack()
        self.x2T = esX2.enter_context(
            nc.sbuf_tensor("x2T", [128, ND, S2], BF16, side="right"))
        self.q_sb = esQ.enter_context(
            nc.sbuf_tensor("q_sb", [128, NTO, D], BF16, side="right"))
        esQT = contextlib.ExitStack()
        with contextlib.ExitStack() as ps_es, nc.Block() as blk:
            self._mk_psum(ps_es)
            @blk.gpsimd
            def _(g):
                g.dma_start(
                    out=self.x2T[:, :, 0:OCW],
                    in_=self.w_oT.rearrange("(dc p) o -> p dc o", p=128)
                    [:, :, 0:OCW]).then_inc(self.sow.h, 16)
                self.sow.n += 16
            self.phase_proj(blk, 0, "q")
            esA.close()
            self.qT = esQT.enter_context(
                nc.sbuf_tensor("qT", [128, ND, S2], BF16))
            self.phase_rope_q_transpose(blk)
            for t in range(2):
                xs = self.x_scr[t]
                sxp = self.sxp[t]

                @blk.sync
                def _(s, t=t, xs=xs, sxp=sxp):
                    self._wait_free(s, f"x_scr{t}")
                    if t == 0:
                        s.wait_ge(self.sa.h, self.a_store_done)
                    s.dma_start(out=xs[:, :],
                                in_=self.x_all[t * 128:(t + 1) * 128, :]
                                ).then_inc(sxp.h, 16)
                    sxp.n += 16
                self.x1init_dma = getattr(self, "x1init_dma", {})
                self.x1init_dma[t] = sxp.n
        esQ.close()
        esX1 = contextlib.ExitStack()
        self.x1 = esX1.enter_context(
            nc.sbuf_tensor("x1", [128, NTO, D], F32, side="right"))
        with contextlib.ExitStack() as ps_es, nc.Block() as blk:
            self._mk_psum(ps_es, n_m=4)
            ps_l = ps_es.enter_context(
                nc.psum_tensor(self._nm("ps_l"), [128, E], F32))
            ps_gt = ps_es.enter_context(
                nc.psum_tensor(self._nm("ps_gt"), [self.NTO * 8, 128], BF16))
            self.phase_x1init(blk)
            self._emit_wdma(blk, "o", 1)
            self.phase_oproj(blk, ps_l)
            self.phase_gates(blk, ps_gt)
            self.phase_experts(blk)
            self.phase_biasmix(blk)
        self.phase_store()
        esX1.close()
        esQT.close()
        esX2.close()
        es0.close()
        return self.nc

    def _mk_psum(self, ps_es, tr=True, n_m=2):
        nc, OCW = self.nc, self.OCW
        if tr:
            self.ps_tr = [ps_es.enter_context(
                nc.psum_tensor(self._nm("ps_tr") + str(i), [128, 1024], BF16))
                for i in range(2)]
            for i in range(2):
                self.free.pop(f"tr{i}", None)
        self.ps_m = [ps_es.enter_context(
            nc.psum_tensor(self._nm("ps_m") + str(i), [128, OCW], F32))
            for i in range(n_m)]
        for i in range(4):
            self.free.pop(f"psm{i}", None)

    def phase0(self):
        nc = self.nc
        s0w, sg = self.s0w, self.sg
        with nc.Block() as blk:
            @blk.gpsimd
            def _(g):
                g.dma_start(out=self.gw_sb[:, :, :],
                            in_=self.g_wT.rearrange("(dc p) e -> p dc e", p=128)
                            ).then_inc(s0w.h, 16)
                s0w.n += 16
                g.dma_start(out=self.eb_sb[:, :], in_=self.e_b[:, :]
                            ).then_inc(s0w.h, 16)
                s0w.n += 16
                g.iota(self.iota_f[:, :], [[1, 128]], channel_multiplier=0,
                       allow_small_or_imprecise_dtypes=True)
                g.iota(self.iota_p[:, :1], [[1, 1]], channel_multiplier=1,
                       allow_small_or_imprecise_dtypes=True)
                sg.inc(g.memset(self.ones_c[:, :], 1.0))

            @blk.vector
            def _(v):
                v.wait_ge(sg.h, sg.n)
                v.tensor_scalar(out=self.ident_bf[:, :], in0=self.iota_f[:, :],
                                scalar1=self.iota_p[:, 0:1], scalar2=None,
                                op0=ALU.is_equal)

    def _ln_stats_tile(self, blk, stats, x_in):
        D = self.D
        sbuf, si = stats
        nch = D // 512 if D >= 512 else 1
        cw = D // nch

        @blk.vector
        def _(v):
            v.drain()
            for c in range(nch):
                v.bn_stats(out=sbuf[:, si, 6 * c:6 * c + 6],
                           in_=x_in[:, c * cw:(c + 1) * cw])
            v.drain()
            v.bn_aggr(out=sbuf[:, si, 24:26], in_=sbuf[:, si, 0:6 * nch])

    def _ln_chain(self, blk, sbuf, lo, n):
        sc, sv = self.sc, self.sv
        mean = sbuf[:, lo:lo + n, 24:25]
        var = sbuf[:, lo:lo + n, 25:26]
        vpe = sbuf[:, lo:lo + n, 26:27]
        recip = sbuf[:, lo:lo + n, 27:28]
        rstd = sbuf[:, lo:lo + n, 28:29]
        negmr = sbuf[:, lo:lo + n, 29:30]
        opr = sbuf[:, lo:lo + n, 30:31]

        @blk.vector
        def _(v):
            v.drain()
            v.tensor_scalar(out=vpe, in0=var, scalar1=LN_EPS, scalar2=None,
                            op0=ALU.add)
            v.drain()
            sv.inc(v.reciprocal(out=recip, in_=vpe))
        recip_done = sv.n

        @blk.scalar
        def _(s):
            s.wait_ge(sv.h, recip_done)
            sc.inc(s.activation(out=rstd, in_=recip, func=AF.Sqrt))
        sqrt_done = sc.n

        @blk.vector
        def _(v):
            v.wait_ge(sc.h, sqrt_done)
            v.scalar_tensor_tensor(out=negmr, in0=mean, scalar=-1.0, in1=rstd,
                                   op0=ALU.mult, op1=ALU.mult)
            sv.inc(v.tensor_scalar(out=opr, in0=rstd, scalar1=1.0,
                                   scalar2=None, op0=ALU.add))
        return sv.n

    def _ln_apply(self, blk, stats, x_in, out_bf, ready, xkey):
        sv = self.sv
        sbuf, si = stats

        @blk.vector
        def _(v):
            v.wait_ge(sv.h, ready)
            self._wait_free(v, xkey)
            v.drain()
            sv.inc(v.tensor_scalar(out=out_bf, in0=x_in,
                                   scalar1=sbuf[:, si, 28:29],
                                   scalar2=sbuf[:, si, 29:30],
                                   op0=ALU.mult, op1=ALU.add))
        return sv.n

    def _transpose_tile(self, blk, src_of, dstT, tblock, scalar_evac=True,
                        scale_of=None):
        ND = self.ND
        sp, sv = self.sp, self.sv
        nb = (ND + 7) // 8
        for b in range(nb):
            ps = self.ps_tr[b % 2]
            lo = b * 8
            hi = min(lo + 8, ND)

            @blk.tensor
            def _(t, ps=ps, lo=lo, hi=hi, b=b):
                self._wait_free(t, f"tr{b % 2}")
                for dc in range(lo, hi):
                    tr = t.transpose(ps[:, (dc - lo) * 128:(dc - lo + 1) * 128],
                                     src_of(dc), self.ident_bf[:, :])
                sp.inc(tr)
            tr_done = self.sp.n
            if scale_of is not None:
                if b % 2 == 0:
                    @blk.vector
                    def _(v, ps=ps, lo=lo, hi=hi, tr_done=tr_done, b=b):
                        v.wait_ge(sp.h, tr_done)
                        if tblock == 0 and b == 0:
                            v.wait_ge(self.sa.h, self.aT_load_done)
                        for dc in range(lo, hi):
                            ins = v.tensor_scalar(
                                out=dstT[:, dc,
                                         tblock * 128:(tblock + 1) * 128],
                                in0=ps[:, (dc - lo) * 128:(dc - lo + 1) * 128],
                                scalar1=scale_of(dc), scalar2=None,
                                op0=ALU.mult)
                        sv.inc(ins)
                    self.free[f"tr{b % 2}"] = (sv, sv.n)
                else:
                    @blk.scalar
                    def _(s, ps=ps, lo=lo, hi=hi, tr_done=tr_done):
                        s.wait_ge(sp.h, tr_done)
                        if tblock == 0:
                            s.wait_ge(self.sa.h, self.aT_load_done)
                        for dc in range(lo, hi):
                            ins = s.activation(
                                out=dstT[:, dc,
                                         tblock * 128:(tblock + 1) * 128],
                                in_=ps[:, (dc - lo) * 128:(dc - lo + 1) * 128],
                                func=AF.Identity, scale=scale_of(dc))
                        self.sc.inc(ins)
                    self.free[f"tr{b % 2}"] = (self.sc, self.sc.n)
            elif b % 2 == 0 or not scalar_evac:
                @blk.vector
                def _(v, ps=ps, lo=lo, hi=hi, tr_done=tr_done):
                    v.wait_ge(sp.h, tr_done)
                    sv.inc(v.tensor_copy(
                        out=dstT[:, lo:hi, tblock * 128:(tblock + 1) * 128],
                        in_=ps[:, 0:(hi - lo) * 128]))
                self.free[f"tr{b % 2}"] = (sv, sv.n)
            else:
                @blk.scalar
                def _(s, ps=ps, lo=lo, hi=hi, tr_done=tr_done):
                    s.wait_ge(sp.h, tr_done)
                    self.sc.inc(s.activation(
                        out=dstT[:, lo:hi, tblock * 128:(tblock + 1) * 128],
                        in_=ps[:, 0:(hi - lo) * 128], func=AF.Copy))
                self.free[f"tr{b % 2}"] = (self.sc, self.sc.n)
        self.free["xln_scr_last_tr"] = (sp, self.sp.n)
        self.tile_ready[tblock] = (self.sv.n, self.sc.n)

    def phase_ln1(self, blk, sp_i, after_tile=None):
        NTO = self.NTO
        sc, sv = self.sc, self.sv
        self.tile_ready = {}
        BL = 2
        for t0 in range(0, NTO, BL):
            for t in range(t0, t0 + BL):
                g = sp_i * NTO + t
                xs = self.x_scr[t % 2]
                sxp = self.sxp[t % 2]

                @blk.sync
                def _(s, t=t, g=g, xs=xs, sxp=sxp):
                    self._wait_free(s, f"x_scr{t % 2}")
                    s.dma_start(out=xs[:, :],
                                in_=self.x_all[g * 128:(g + 1) * 128, :]
                                ).then_inc(sxp.h, 16)
                    sxp.n += 16
                dma_done = sxp.n

                @blk.vector
                def _(v, dma_done=dma_done, sxp=sxp):
                    v.wait_ge(sxp.h, dma_done)
                self._ln_stats_tile(blk, (self.st1, g), xs[:, :])
            ready = self._ln_chain(blk, self.st1, sp_i * NTO + t0, BL)
            for t in range(t0, t0 + BL):
                g = sp_i * NTO + t
                xs = self.x_scr[t % 2]
                xln = self.xln_scr[t % 2]
                apply_done = self._ln_apply(blk, (self.st1, g), xs[:, :],
                                            xln[:, :], ready, f"xlnf{t % 2}")
                self.free[f"x_scr{t % 2}"] = (sv, apply_done)

                @blk.tensor
                def _(te, apply_done=apply_done):
                    te.wait_ge(sv.h, apply_done)
                self._transpose_tile(
                    blk, lambda dc, xln=xln: xln[:, dc * 128:(dc + 1) * 128],
                    self.xlnT, t)
                self.free[f"xlnf{t % 2}"] = (self.sp, self.sp.n)
                if after_tile is not None:
                    after_tile(t)

    def _emit_wdma(self, blk, which, q):
        OCW = self.OCW
        wT = {"q": self.w_qT, "k": self.w_kT, "v": self.w_vT,
              "o": self.w_oT}[which]
        swp = self.swp[q % 2]

        @blk.gpsimd
        def _(g, q=q, swp=swp, wT=wT):
            self._wait_free(g, f"wbuf{q % 2}")
            g.dma_start(
                out=self.wbuf[q % 2][:, :, :],
                in_=wT.rearrange("(dc p) o -> p dc o", p=128)
                [:, :, q * OCW:(q + 1) * OCW]).then_inc(swp.h, 16)
            swp.n += 16
        self.wdma_done[(which, q)] = swp.n

    def _get_wdma(self, blk, which, q):
        if (which, q) not in self.wdma_done:
            self._emit_wdma(blk, which, q)
        return self.wdma_done.pop((which, q))

    def _proj_bias(self, blk, which):
        brow = {"q": 0, "k": 1, "v": 2, "o": 3}[which]
        sbb = self.sbb

        @blk.gpsimd
        def _(g, sbb=sbb, brow=brow):
            self._wait_free(g, "bb_scr")
            g.dma_start(out=self.bb_scr[:, :],
                        in_=self.biases[brow:brow + 1, :]
                        .partition_broadcast(128)).then_inc(sbb.h, 16)
            sbb.n += 16
        self.bias_done = sbb.n

    def _proj_qt(self, blk, which, q, t, w_done, first_t):
        NTO, OCW, ND = self.NTO, self.OCW, self.ND
        sc, sv, sp = self.sc, self.sv, self.sp
        dst = {"k": getattr(self, "k_sb", None),
               "v": getattr(self, "v_sb", None),
               "q": getattr(self, "q_sb", None)}[which]
        bias_done = self.bias_done
        self.psm_ctr += 1
        nm_ = len(self.ps_m)
        ps = self.ps_m[self.psm_ctr % nm_]
        pnm = f"psm{self.psm_ctr % nm_}"
        tile_rdy = self.tile_ready.get(t)

        @blk.tensor
        def _(te, q=q, t=t, ps=ps, w_done=w_done, pnm=pnm):
            te.wait_ge(self.swp[q % 2].h, w_done)
            if q == 0 and tile_rdy is not None:
                te.wait_ge(sv.h, tile_rdy[0])
                te.wait_ge(self.sc.h, tile_rdy[1])
            self._wait_free(te, pnm)
            for dc in range(ND):
                mm = te.matmul(
                    ps[:, :],
                    self.xlnT[:, dc, t * 128:(t + 1) * 128],
                    self.wbuf[q % 2][:, dc, :],
                    start=(dc == 0), stop=(dc == ND - 1))
            sp.inc(mm)
        mmc = sp.n

        @blk.vector
        def _(v, q=q, t=t, ps=ps, mmc=mmc, bias_done=bias_done):
            v.wait_ge(sp.h, mmc)
            v.wait_ge(self.sbb.h, bias_done)
            sv.inc(v.tensor_add(
                out=dst[:, t, q * OCW:(q + 1) * OCW],
                in0=ps[:, :],
                in1=self.bb_scr[:, q * OCW:(q + 1) * OCW]))
        self.free[pnm] = (sv, sv.n)
        self.kv_evac[(which, t)] = sv.n

    def phase_proj(self, blk, sp_i, which, after_quarter=None, start_q=0):
        NTO, NQ = self.NTO, self.NQ
        sp = self.sp
        if start_q == 0:
            self._proj_bias(blk, which)
        for q in range(start_q, NQ):
            w_done = self._get_wdma(blk, which, q)
            for t in range(NTO):
                self._proj_qt(blk, which, q, t, w_done, first_t=(t == 0))
            self.free[f"wbuf{q % 2}"] = (sp, sp.n)
            if after_quarter is not None:
                after_quarter(q)
        self.free["bb_scr"] = (self.sv, self.sv.n)

    def _rope_tile(self, blk, buf, t, g, which):
        RH = self.RH
        sv = self.sv
        sct = self.sct[t % 2]
        ct = self.ct_scr[t % 2]
        sn = self.sn_scr[t % 2]

        @blk.sync
        def _(s, t=t, g=g, sct=sct):
            self._wait_free(s, f"ct{t % 2}")
            s.dma_start(out=ct[:, :],
                        in_=self.cos_t[g * 128:(g + 1) * 128, :]
                        ).then_inc(sct.h, 16)
            s.dma_start(out=sn[:, :],
                        in_=self.sin_t[g * 128:(g + 1) * 128, :]
                        ).then_inc(sct.h, 16)
            sct.n += 32
        tab_done = sct.n

        @blk.vector
        def _(v, t=t, tab_done=tab_done, sct=sct):
            v.wait_ge(sct.h, tab_done)
            self._wait_free(v, "tmp8")
            v.drain()
            ev = buf[:, t, 0:RH]
            od = buf[:, t, RH:2 * RH]
            rt = [self.tmp8[:, i, :] for i in range(4)]
            v.tensor_mul(out=rt[0], in0=ev, in1=ct[:, :])
            v.tensor_mul(out=rt[1], in0=od, in1=sn[:, :])
            v.tensor_mul(out=rt[2], in0=od, in1=ct[:, :])
            v.tensor_mul(out=rt[3], in0=ev, in1=sn[:, :])
            v.drain()
            v.tensor_sub(out=ev, in0=rt[0], in1=rt[1])
            sv.inc(v.tensor_add(out=od, in0=rt[2], in1=rt[3]))
        self.free[f"ct{t % 2}"] = (sv, sv.n)
        self.free["tmp8"] = (sv, sv.n)

    def _rope_norm_k_tile(self, blk, sp_i, t):
        sv = self.sv
        g = sp_i * self.NTO + t
        self._rope_tile(blk, self.k_sb, t, g, "k")

        @blk.vector
        def _(v, t=t, g=g):
            self._wait_free(v, "tmp8")
            v.drain()
            sv.inc(v.scalar_tensor_tensor(
                out=self.tmp8.rearrange("p a b -> p (a b)"),
                in0=self.k_sb[:, t, :], scalar=1.0,
                in1=self.k_sb[:, t, :], op0=ALU.mult, op1=ALU.mult,
                accum_out=self.kst[:, g, 0:1]))
        self.free["tmp8"] = (sv, sv.n)
        self.ssq_done = getattr(self, "ssq_done", {})
        self.ssq_done[g] = sv.n

    def _rnorm_batch(self, blk, g0, lo, hi):
        sv = self.sv
        ssq = self.kst[:, g0 + lo:g0 + hi, 0:1]
        nrm = self.kst[:, g0 + lo:g0 + hi, 1:2]
        clm = self.kst[:, g0 + lo:g0 + hi, 2:3]
        rnm = self.kst[:, g0 + lo:g0 + hi, 3:4]
        ssq_all = self.ssq_done[g0 + hi - 1]

        @blk.scalar
        def _(s, ssq_all=ssq_all, ssq=ssq, nrm=nrm):
            s.wait_ge(sv.h, ssq_all)
            self.sc.inc(s.activation(out=nrm, in_=ssq, func=AF.Sqrt))
        nrm_done = self.sc.n

        @blk.vector
        def _(v, nrm_done=nrm_done, nrm=nrm, clm=clm, rnm=rnm, lo=lo, hi=hi):
            v.wait_ge(self.sc.h, nrm_done)
            v.tensor_scalar(out=clm, in0=nrm, scalar1=NORM_EPS, scalar2=None,
                            op0=ALU.max)
            v.drain()
            v.reciprocal(out=rnm, in_=clm)
            v.drain()
            sv.inc(v.tensor_copy(out=self.rn_bf[:, lo:hi], in_=rnm))
        return sv.n

    def phase_pa(self, blk, sp_i, ps_a, late=()):
        NTO, NQ, OCW = self.NTO, self.NQ, self.OCW
        sv, sp = self.sv, self.sp
        first = sp_i == 1
        last = sp_i == 0
        late = set(late)
        order = [t for t in range(NTO) if t not in late] + sorted(late)
        g0 = sp_i * NTO
        n_early = NTO - len(late)
        rn_early = self._rnorm_batch(blk, g0, 0, n_early) if n_early else None
        rn_of = lambda t: rn_early if t < n_early else rn_late

        def pa_tile(t):
            is_late = t in late
            @blk.vector
            def _(v, t=t, is_late=is_late):
                if t == order[0] or is_late:
                    v.drain()
                sv.inc(v.tensor_mul(out=self.v_sb[:, t, :],
                                    in0=self.k_sb[:, t, :],
                                    in1=self.v_sb[:, t, :]))
            p_done = sv.n

            is_first = first and t == order[0]
            is_last = last and t == order[-1]

            @blk.tensor
            def _(te, t=t, p_done=p_done, is_first=is_first, is_last=is_last,
                  rn_done=rn_of(t)):
                te.wait_ge(sv.h, p_done)
                te.wait_ge(sv.h, rn_done)
                for qi in range(NQ):
                    mm = te.matmul(ps_a[qi][:, :], self.rn_bf[:, t:t + 1],
                                   self.v_sb[:, t, qi * OCW:(qi + 1) * OCW],
                                   start=is_first, stop=is_last)
                if is_last:
                    sp.inc(mm)

        for t in order[:n_early]:
            pa_tile(t)
        for t in sorted(late):
            self._rope_norm_k_tile(blk, sp_i, t)
        rn_late = self._rnorm_batch(blk, g0, n_early, NTO) if late else None
        for t in order[n_early:]:
            pa_tile(t)
        if last:
            a_done = self.sp.n

            @blk.vector
            def _(v):
                v.wait_ge(sp.h, a_done)
                self._wait_free(v, "x_scr0")
                v.drain()
                for qi in range(NQ):
                    ins = v.tensor_copy(
                        out=self.x_scr[0][0:1, qi * OCW:(qi + 1) * OCW],
                        in_=ps_a[qi][:, :])
                self.sv.inc(ins)
            self.free["x_scr0"] = (self.sv, self.sv.n)
            self.a_row_ready = self.sv.n

    def phase_a_roundtrip(self, blk):
        sa, sv = self.sa, self.sv
        a_done = self.a_row_ready

        @blk.sync
        def _(s):
            s.wait_ge(sv.h, a_done)
            s.dma_start(out=self.a_hbm[None, :], in_=self.x_scr[0][0:1, :]
                        ).then_inc(sa.h, 16)
            sa.n += 16
            s.wait_ge(sa.h, sa.n)
            self.a_store_done = sa.n
            s.dma_start(out=self.aT[:, :],
                        in_=self.a_hbm.rearrange("(dc p) -> p dc", p=128),
                        allow_slow_non_contiguous=True,
                        ).then_inc(sa.h, 16)
            sa.n += 16
        self.aT_load_done = sa.n

    def phase_rope_q_transpose(self, blk):
        ND, NTO = self.ND, self.NTO
        sa, sv = self.sa, self.sv
        self.tile_ready = {}
        for t in range(NTO):
            self._rope_tile(blk, self.q_sb, t, t, "q")
            rope_done = sv.n

            @blk.tensor
            def _(te, rope_done=rope_done):
                te.wait_ge(sv.h, rope_done)
            self._transpose_tile(
                blk, lambda dc, t=t: self.q_sb[:, t, dc * 128:(dc + 1) * 128],
                self.qT, t,
                scale_of=lambda dc: self.aT[:, dc:dc + 1])

    def _scale_wquarter(self, blk, which, q):
        ND = self.ND
        sv, sa = self.sv, self.sa
        swp = self.swp[q % 2]
        w_done = self._get_wdma(blk, which, q)

        @blk.vector
        def _(v, q=q, w_done=w_done, swp=swp):
            v.wait_ge(swp.h, w_done)
            v.wait_ge(sa.h, self.aT_load_done)
            for dc in range(ND):
                ins = v.tensor_scalar(out=self.wbuf[q % 2][:, dc, :],
                                      in0=self.wbuf[q % 2][:, dc, :],
                                      scalar1=self.aT[:, dc:dc + 1],
                                      scalar2=None, op0=ALU.mult)
            sv.inc(ins)
        self.wscale_done = getattr(self, "wscale_done", {})
        self.wscale_done[q] = (sv.n, w_done)

    def phase_x1init(self, blk):
        NTO = self.NTO
        sv = self.sv
        self._proj_bias(blk, "o")
        for t in range(NTO):
            xs = self.x_scr[t % 2]
            sxp = self.sxp[t % 2]
            if t < 2:
                dma_done = self.x1init_dma[t]
            else:
                @blk.sync
                def _(s, t=t, xs=xs, sxp=sxp):
                    self._wait_free(s, f"x_scr{t % 2}")
                    s.dma_start(out=xs[:, :],
                                in_=self.x_all[t * 128:(t + 1) * 128, :]
                                ).then_inc(sxp.h, 16)
                    sxp.n += 16
                dma_done = sxp.n

            @blk.vector
            def _(v, t=t, xs=xs, dma_done=dma_done, sxp=sxp):
                v.wait_ge(sxp.h, dma_done)
                if t == 0:
                    v.wait_ge(self.sbb.h, self.bias_done)
                v.drain()
                v.tensor_scalar(out=self.x1[:, t, :], in0=xs[:, :],
                                scalar1=self.st1[:, t, 30:31],
                                scalar2=self.st1[:, t, 29:30],
                                op0=ALU.mult, op1=ALU.add)
                v.drain()
                sv.inc(v.tensor_add(out=self.x1[:, t, :],
                                    in0=self.x1[:, t, :],
                                    in1=self.bb_scr[:, :]))
            self.free[f"x_scr{t % 2}"] = (sv, sv.n)

    def phase_oproj(self, blk, ps_l):
        NTO, NQ, OCW, ND = self.NTO, self.NQ, self.OCW, self.ND
        sv, sp = self.sv, self.sp
        self.tile_ready = {}
        for q in range(NQ):
            if q == 0:
                swp, w_done = self.sow, 16
                rhs_of = lambda dc: self.x2T[:, dc, 0:OCW]
            else:
                swp = self.swp[q % 2]
                w_done = self._get_wdma(blk, "o", q)
                rhs_of = lambda dc, q=q: self.wbuf[q % 2][:, dc, :]
            for t in range(NTO):
                nm_ = len(self.ps_m)
                ps = self.ps_m[(q * NTO + t) % nm_]
                pnm = f"psm{(q * NTO + t) % nm_}"

                @blk.tensor
                def _(te, q=q, t=t, ps=ps, w_done=w_done, swp=swp, pnm=pnm,
                      rhs_of=rhs_of):
                    te.wait_ge(swp.h, w_done)
                    self._wait_free(te, pnm)
                    for dc in range(ND):
                        mm = te.matmul(
                            ps[:, :],
                            self.qT[:, dc, t * 128:(t + 1) * 128],
                            rhs_of(dc),
                            start=(dc == 0), stop=(dc == ND - 1))
                    sp.inc(mm)
                mmc = sp.n

                @blk.vector
                def _(v, q=q, t=t, ps=ps, mmc=mmc):
                    v.wait_ge(sp.h, mmc)
                    if q == 0:
                        v.drain()
                    sv.inc(v.scalar_tensor_tensor(
                        out=self.x1[:, t, q * OCW:(q + 1) * OCW],
                        in0=ps[:, :], scalar=1.0,
                        in1=self.x1[:, t, q * OCW:(q + 1) * OCW],
                        op0=ALU.mult, op1=ALU.add))
                self.free[pnm] = (sv, sv.n)
                if q == NQ - 1:
                    self._ln2_tile(blk, ps_l, t)
            if q != 0:
                self.free[f"wbuf{q % 2}"] = (sp, sp.n)
        self.free["bb_scr"] = (sv, self.sv.n)

    def _ln2_tile(self, blk, ps_l, t):
        self._ln_stats_tile(blk, (self.st2, t), self.x1[:, t, :])
        self._ln2_pend.append(t)
        if len(self._ln2_pend) == 4 or t == self.NTO - 1:
            tiles = self._ln2_pend
            self._ln2_pend = []
            ready = self._ln_chain(blk, self.st2, tiles[0], len(tiles))
            for tt in tiles:
                self._ln2_tail(blk, ps_l, tt, ready)

    def _ln2_tail(self, blk, ps_l, t, ready):
        ND = self.ND
        sc, sv, sp, s0w = self.sc, self.sv, self.sp, self.s0w
        xln = self.xln_scr[t % 2]
        apply_done = self._ln_apply(blk, (self.st2, t), self.x1[:, t, :],
                                    xln[:, :], ready, f"xlnf{t % 2}")

        @blk.tensor
        def _(te, apply_done=apply_done):
            te.wait_ge(sv.h, apply_done)
        self._transpose_tile(
            blk, lambda dc, xln=xln: xln[:, dc * 128:(dc + 1) * 128],
            self.x2T, t, scalar_evac=False)
        self.free[f"xlnf{t % 2}"] = (self.sp, self.sp.n)
        x2T_done = self.tile_ready[t]

        @blk.tensor
        def _(te, t=t, x2T_done=x2T_done):
            te.wait_ge(s0w.h, 32)
            self._wait_free(te, "ps_l")
            te.wait_ge(sv.h, x2T_done[0])
            te.wait_ge(sc.h, x2T_done[1])
            for dc in range(ND):
                mm = te.matmul(ps_l[:, :],
                               self.x2T[:, dc, t * 128:(t + 1) * 128],
                               self.gw_sb[:, dc, :],
                               start=(dc == 0), stop=(dc == ND - 1))
            self.sp.inc(mm)
        lg_done = self.sp.n

        @blk.vector
        def _(v, t=t, lg_done=lg_done):
            v.wait_ge(sp.h, lg_done)
            sv.inc(v.tensor_copy(out=self.logit[:, t, :], in_=ps_l[:, :]))
        self.free["ps_l"] = (sv, sv.n)

    def phase_gates(self, blk, ps_gt):
        NTO, E = self.NTO, self.E
        sc, sv, sp = self.sc, self.sv, self.sp
        L = self.logit[:, :, :]
        m1 = self.gst[:, :, 0:1]
        m2 = self.gst[:, :, 1:2]
        w2 = self.gst[:, :, 2:3]
        w1 = self.gst[:, :, 3:4]

        def bc(ap):
            return ap.broadcast_to([128, NTO, E])

        @blk.vector
        def _(v):
            v.drain()
            v.tensor_reduce(out=m1, in_=L, axis=AXX, op=ALU.max)
            v.drain()
            v.tensor_tensor(out=self.eq1s[:, :, :], in0=L, in1=bc(m1),
                            op=ALU.is_equal)
            v.drain()
            v.scalar_tensor_tensor(out=self.l2s[:, :, :],
                                   in0=self.eq1s[:, :, :],
                                   scalar=-BIG_NEG, in1=L,
                                   op0=ALU.mult, op1=ALU.add)
            v.drain()
            v.tensor_reduce(out=m2, in_=self.l2s[:, :, :], axis=AXX,
                            op=ALU.max)
            v.drain()
            sv.inc(v.tensor_tensor(out=w2, in0=m2, in1=m1, op=ALU.subtract))
        d_done = sv.n

        @blk.scalar
        def _(s, d_done=d_done):
            s.wait_ge(sv.h, d_done)
            sc.inc(s.activation(out=w2, in_=w2, func=AF.Sigmoid))
        sig_done = sc.n

        @blk.vector
        def _(v, sig_done=sig_done):
            v.wait_ge(sc.h, sig_done)
            v.tensor_scalar(out=w1, in0=w2, scalar1=-1.0, scalar2=1.0,
                            op0=ALU.mult, op1=ALU.add)
            v.drain()
            v.tensor_tensor(out=self.l2s[:, :, :], in0=self.l2s[:, :, :],
                            in1=bc(m2), op=ALU.is_equal)
            v.drain()
            v.tensor_tensor(out=self.l2s[:, :, :], in0=self.l2s[:, :, :],
                            in1=bc(w2), op=ALU.mult)
            v.tensor_tensor(out=self.eq1s[:, :, :], in0=self.eq1s[:, :, :],
                            in1=bc(w1), op=ALU.mult)
            v.drain()
            v.tensor_tensor(out=self.gates[:, :, :], in0=self.eq1s[:, :, :],
                            in1=self.l2s[:, :, :], op=ALU.add)
            v.drain()
            sv.inc(v.tensor_copy(out=self.gb_scr[:, :, :],
                                 in_=self.gates[:, :, :]))
        g_done = sv.n

        @blk.tensor
        def _(te, g_done=g_done):
            te.wait_ge(sv.h, g_done)
            self.sp.inc(te.transpose(
                ps_gt[:, :],
                self.gb_scr.rearrange("p t e -> p (t e)"),
                self.ident_bf[:, :]))
        gt_done = self.sp.n

        gpk = self.tmp8.rearrange("p a b -> p (a b)")[0:NTO * E, 0:128]

        @blk.vector
        def _(v, gt_done=gt_done):
            v.wait_ge(sp.h, gt_done)
            self._wait_free(v, "tmp8")
            sv.inc(v.tensor_copy(out=gpk, in_=ps_gt[:, :]))
        self.free["ps_gt_ev"] = (sv, sv.n)
        pk_done = sv.n

        @blk.sync
        def _(s, pk_done=pk_done, gpk=gpk):
            s.wait_ge(sv.h, pk_done)
            s.dma_start(out=self.gt_hbm[:, :], in_=gpk
                        ).then_inc(self.sgt.h, 16)
            self.sgt.n += 16
            s.wait_ge(self.sgt.h, self.sgt.n)
            s.dma_start(out=self.gTb[:, :, :],
                        in_=self.gt_hbm.rearrange("(t e) n -> e t n", e=8)
                        ).then_inc(self.sgt.h, 16)
            self.sgt.n += 16
        self.free["ps_gt"] = (self.sgt, self.sgt.n)

    def phase_biasmix(self, blk):
        NTO, NQ, OCW = self.NTO, self.NQ, self.OCW
        sv, sp, s0w = self.sv, self.sp, self.s0w
        so = self.so
        gTb_sem, gTb_done = self.free["ps_gt"]
        for t in range(NTO):
            for q in range(NQ):
                nm_ = len(self.ps_m)
                ps = self.ps_m[(t * NQ + q) % nm_]
                pnm = f"psm{(t * NQ + q) % nm_}"

                @blk.tensor
                def _(te, t=t, q=q, ps=ps, pnm=pnm):
                    te.wait_ge(s0w.h, 32)
                    if (t, q) == (0, 0):
                        te.wait_ge(gTb_sem.h, gTb_done)
                    self._wait_free(te, pnm)
                    self.sp.inc(te.matmul(
                        ps[:, :], self.gTb[:, t, :],
                        self.eb_sb[:, q * OCW:(q + 1) * OCW],
                        start=True, stop=True))
                mm_done = self.sp.n

                @blk.vector
                def _(v, t=t, q=q, ps=ps, mm_done=mm_done):
                    v.wait_ge(sp.h, mm_done)
                    if (t, q) == (0, 0):
                        v.drain()
                    sv.inc(v.scalar_tensor_tensor(
                        out=self.x1[:, t, q * OCW:(q + 1) * OCW],
                        in0=ps[:, :], scalar=1.0,
                        in1=self.x1[:, t, q * OCW:(q + 1) * OCW],
                        op0=ALU.mult, op1=ALU.add))
                self.free[pnm] = (sv, sv.n)
            tile_done = sv.n

            @blk.sync
            def _(s, t=t, tile_done=tile_done):
                s.wait_ge(sv.h, tile_done)
                s.dma_start(out=self.y[t * 128:(t + 1) * 128, :],
                            in_=self.x1[:, t, :]).then_inc(so.h, 16)
                so.n += 16

    def phase_experts(self, blk):
        ND, NTO, NQ, OCW, E = self.ND, self.NTO, self.NQ, self.OCW, self.E
        sv, sp = self.sv, self.sp
        steps = [(e_i, q) for e_i in range(E) for q in range(NQ)]
        for si, (e_i, q) in enumerate(steps):
            swp = self.swp[si % 2]

            @blk.gpsimd
            def _(g, si=si, e_i=e_i, q=q, swp=swp):
                self._wait_free(g, f"wbuf{si % 2}")
                g.dma_start(
                    out=self.wbuf[si % 2][:, :, :],
                    in_=self.e_w.rearrange("e (dc p) o -> e p dc o", p=128)
                    [e_i, :, :, q * OCW:(q + 1) * OCW]).then_inc(swp.h, 16)
                swp.n += 16
            w_done = swp.n
            for t in range(NTO):
                nm_ = len(self.ps_m)
                ps = self.ps_m[(si * NTO + t) % nm_]
                pnm = f"psm{(si * NTO + t) % nm_}"

                x2rdy = self.tile_ready[t]

                @blk.tensor
                def _(te, si=si, t=t, ps=ps, w_done=w_done, swp=swp, pnm=pnm,
                      x2rdy=x2rdy):
                    te.wait_ge(swp.h, w_done)
                    if si == 0:
                        te.wait_ge(sv.h, x2rdy[0])
                        te.wait_ge(self.sc.h, x2rdy[1])
                    self._wait_free(te, pnm)
                    for dc in range(ND):
                        mm = te.matmul(
                            ps[:, :],
                            self.x2T[:, dc, t * 128:(t + 1) * 128],
                            self.wbuf[si % 2][:, dc, :],
                            start=(dc == 0), stop=(dc == ND - 1))
                    self.sp.inc(mm)
                mmc = self.sp.n

                @blk.vector
                def _(v, e_i=e_i, q=q, t=t, ps=ps, mmc=mmc):
                    v.wait_ge(sp.h, mmc)
                    if t == 0:
                        v.drain()
                    sv.inc(v.scalar_tensor_tensor(
                        out=self.x1[:, t, q * OCW:(q + 1) * OCW],
                        in0=ps[:, :], scalar=self.gates[:, t, e_i:e_i + 1],
                        in1=self.x1[:, t, q * OCW:(q + 1) * OCW],
                        op0=ALU.mult, op1=ALU.add))
                self.free[pnm] = (sv, sv.n)
            self.free[f"wbuf{si % 2}"] = (sp, self.sp.n)

    def phase_store(self):
        so = self.so
        with self.nc.Block() as blk:
            @blk.sync
            def _(s):
                s.wait_ge(so.h, so.n)


def _host_prep(L0_attn_W_in, L0_attn_b_in, L0_attn_W_out, L0_attn_b_out,
               L0_ffn_gate_w, L0_ffn_expert_w, L0_ffn_expert_b, S, D):
    rot = D // 2
    perm = np.concatenate([np.arange(0, rot, 2), np.arange(1, rot, 2),
                           np.arange(rot, D)])
    w_q = L0_attn_W_in[0:D][perm]
    w_k = L0_attn_W_in[D:2 * D][perm]
    w_v = L0_attn_W_in[2 * D:3 * D][perm]
    w_qT = np.ascontiguousarray(w_q.T)
    w_kT = np.ascontiguousarray(w_k.T)
    w_vT = np.ascontiguousarray(w_v.T)
    w_oT = np.ascontiguousarray(L0_attn_W_out[:, perm].T)
    biases = np.ascontiguousarray(np.stack([
        L0_attn_b_in[0:D][perm], L0_attn_b_in[D:2 * D][perm],
        L0_attn_b_in[2 * D:3 * D][perm], L0_attn_b_out]).astype(np.float32))
    g_wT = np.ascontiguousarray(L0_ffn_gate_w.T)
    e_w = np.ascontiguousarray(L0_ffn_expert_w)
    e_b = np.ascontiguousarray(L0_ffn_expert_b)
    inv = 1.0 / (10000.0 ** (np.arange(0, rot, 2, dtype=np.float64) / rot))
    ang = np.arange(S, dtype=np.float64)[:, None] * inv[None, :]
    cos_t = np.cos(ang).astype(ml_dtypes.bfloat16)
    sin_t = np.sin(ang).astype(ml_dtypes.bfloat16)
    return dict(w_qT=w_qT, w_kT=w_kT, w_vT=w_vT, w_oT=w_oT, biases=biases,
                g_wT=g_wT, e_w=e_w, e_b=e_b, cos_t=cos_t, sin_t=sin_t)


def _in_maps(X, shared, S, n_cores):
    S2 = S // 2
    maps = []
    for c in range(n_cores):
        b, h = c // 2, c % 2
        xb = np.ascontiguousarray(X[b])
        if h == 0:
            x_all, cos_c, sin_c = xb, shared["cos_t"], shared["sin_t"]
        else:
            x_all = np.ascontiguousarray(
                np.concatenate([xb[S2:], xb[:S2]], axis=0))
            cos_c = np.ascontiguousarray(np.concatenate(
                [shared["cos_t"][S2:], shared["cos_t"][:S2]], axis=0))
            sin_c = np.ascontiguousarray(np.concatenate(
                [shared["sin_t"][S2:], shared["sin_t"][:S2]], axis=0))
        m = dict(shared)
        m["x_all"] = x_all
        m["cos_t"] = cos_c
        m["sin_t"] = sin_c
        maps.append(m)
    return maps


def kernel(L0_attn_X, L0_attn_W_in, L0_attn_b_in, L0_attn_W_out, L0_attn_b_out,
           L0_ffn_gate_w, L0_ffn_expert_w, L0_ffn_expert_b):
    B, S, D = L0_attn_X.shape
    E = L0_ffn_gate_w.shape[0]
    shared = _host_prep(np.asarray(L0_attn_W_in, dtype=np.float32),
                        np.asarray(L0_attn_b_in, dtype=np.float32),
                        np.asarray(L0_attn_W_out, dtype=np.float32),
                        np.asarray(L0_attn_b_out, dtype=np.float32),
                        np.asarray(L0_ffn_gate_w, dtype=np.float32),
                        np.asarray(L0_ffn_expert_w, dtype=np.float32),
                        np.asarray(L0_ffn_expert_b, dtype=np.float32), S, D)
    X = np.asarray(L0_attn_X, dtype=np.float32)
    maps = _in_maps(X, shared, S, N_CORES)
    nc = Builder(S, D, E, 512).build()
    res = run_bass_kernel_spmd(nc, maps, core_ids=list(range(N_CORES)))
    S2 = S // 2
    out = np.empty((B, S, D), dtype=np.float32)
    for c in range(N_CORES):
        b, h = c // 2, c % 2
        out[b, h * S2:(h + 1) * S2] = res.results[c]["y"]
    return out


# version 32
# speedup vs baseline: 1.0077x; 1.0001x over previous
import contextlib
import numpy as np
import ml_dtypes
import concourse.bass as bass
import concourse.mybir as mybir
from concourse.bass_utils import run_bass_kernel_spmd

F32 = mybir.dt.float32
BF16 = mybir.dt.bfloat16
AF = mybir.ActivationFunctionType
ALU = mybir.AluOpType
AXX = mybir.AxisListType.X

N_CORES = 8
BIG_NEG = 1.0e9
LN_EPS = 1.0e-5
NORM_EPS = 1.0e-12


class Sem:
    __slots__ = ("h", "n")

    def __init__(self, h):
        self.h = h
        self.n = 0

    def inc(self, inst, k=1):
        inst.then_inc(self.h, k)
        self.n += k
        return self.n


class Builder:

    def __init__(self, S, D, E, OCW):
        assert D % 128 == 0 and S % 256 == 0 and D % OCW == 0
        self.S, self.D, self.E, self.OCW = S, D, E, OCW
        self.ND = D // 128
        self.NT = S // 128
        self.NTO = self.NT // 2
        self.NQ = D // OCW
        self.RH = D // 4
        self.nc = bass.Bass("TRN2", target_bir_lowering=False)
        self._uid = 0
        self.free = {}
        self.wdma_done = {}
        self.kv_evac = {}
        self.bias_stage_done = {}
        self.psm_ctr = 0
        self._ln2_pend = []

    def _nm(self, base):
        self._uid += 1
        return f"{base}_{self._uid}"

    def _wait_free(self, eng, name):
        fr = self.free.get(name)
        if fr is not None:
            eng.wait_ge(fr[0].h, fr[1])

    def build(self):
        nc = self.nc
        S, D, E, OCW = self.S, self.D, self.E, self.OCW
        ND, NT, NTO, NQ, RH = self.ND, self.NT, self.NTO, self.NQ, self.RH
        S2 = S // 2

        self.x_all = nc.dram_tensor("x_all", [S, D], F32, kind="ExternalInput").ap()
        self.cos_t = nc.dram_tensor("cos_t", [S, RH], BF16, kind="ExternalInput").ap()
        self.sin_t = nc.dram_tensor("sin_t", [S, RH], BF16, kind="ExternalInput").ap()
        self.w_qT = nc.dram_tensor("w_qT", [D, D], F32, kind="ExternalInput").ap()
        self.w_kT = nc.dram_tensor("w_kT", [D, D], F32, kind="ExternalInput").ap()
        self.w_vT = nc.dram_tensor("w_vT", [D, D], F32, kind="ExternalInput").ap()
        self.w_oT = nc.dram_tensor("w_oT", [D, D], F32, kind="ExternalInput").ap()
        self.biases = nc.dram_tensor("biases", [4, D], F32, kind="ExternalInput").ap()
        self.g_wT = nc.dram_tensor("g_wT", [D, E], F32, kind="ExternalInput").ap()
        self.e_w = nc.dram_tensor("e_w", [E, D, D], F32, kind="ExternalInput").ap()
        self.e_b = nc.dram_tensor("e_b", [E, D], F32, kind="ExternalInput").ap()
        self.y = nc.dram_tensor("y", [S2, D], F32, kind="ExternalOutput").ap()
        self.a_hbm = nc.dram_tensor("a_scr", [D], F32).ap()
        self.gt_hbm = nc.dram_tensor("gt_scr", [NTO * E, 128], BF16).ap()

        es0 = contextlib.ExitStack()
        e = es0.enter_context
        self.ident_bf = e(nc.sbuf_tensor("ident_bf", [128, 128], BF16))
        self.iota_f = e(nc.sbuf_tensor("iota_f", [128, 128], F32))
        self.iota_p = e(nc.sbuf_tensor("iota_p", [128, 1], F32))
        self.ones_c = e(nc.sbuf_tensor("ones_c", [128, 1], BF16))
        self.gw_sb = e(nc.sbuf_tensor("gw_sb", [128, ND, E], BF16))
        self.eb_sb = e(nc.sbuf_tensor("eb_sb", [E, D], BF16))
        self.st1 = e(nc.sbuf_tensor("st1", [128, NT, 32], F32))
        self.st2 = e(nc.sbuf_tensor("st2", [128, NTO, 32], F32))
        self.kst = e(nc.sbuf_tensor("kst", [128, NT, 4], F32))
        self.rn_bf = e(nc.sbuf_tensor("rn_bf", [128, 8], BF16))
        self.aT = e(nc.sbuf_tensor("aT", [128, ND], F32))
        self.gates = e(nc.sbuf_tensor("gates", [128, NTO, E], F32))
        self.logit = e(nc.sbuf_tensor("logit", [128, NTO, E], F32))
        self.gst = e(nc.sbuf_tensor("gst", [128, NTO, 8], F32))
        self.eq1s = e(nc.sbuf_tensor("eq1s", [128, NTO, E], F32))
        self.l2s = e(nc.sbuf_tensor("l2s", [128, NTO, E], F32))
        self.gb_scr = e(nc.sbuf_tensor("gb_scr", [128, NTO, E], BF16))
        self.gTb = e(nc.sbuf_tensor("gTb", [8, NTO, 128], BF16))
        self.x_scr = [e(nc.sbuf_tensor(f"x_scr{i}", [128, D], F32)) for i in range(2)]
        self.tmp8 = e(nc.sbuf_tensor("tmp8", [128, 4, RH], BF16))
        self.ct_scr = [e(nc.sbuf_tensor(f"ct{i}", [128, RH], BF16))
                       for i in range(2)]
        self.sn_scr = [e(nc.sbuf_tensor(f"sn{i}", [128, RH], BF16))
                       for i in range(2)]
        self.bb_scr = e(nc.sbuf_tensor("bb_scr", [128, D], BF16, side="right"))
        self.xln_scr = [e(nc.sbuf_tensor(f"xln_scr{i}", [128, D], BF16)) for i in range(2)]
        self.wbuf = [e(nc.sbuf_tensor(f"wbuf{i}", [128, ND, OCW], BF16))
                     for i in range(2)]
        self.sxp = [Sem(e(nc.semaphore(f"sxp{i}"))) for i in range(2)]
        self.sct = [Sem(e(nc.semaphore(f"sct{i}"))) for i in range(2)]
        self.swp = [Sem(e(nc.semaphore(f"swp{i}"))) for i in range(2)]
        self.sbb = Sem(e(nc.semaphore("sbb")))
        self.s0w = Sem(e(nc.semaphore("s0w")))
        self.sa = Sem(e(nc.semaphore("sa")))
        self.sgt = Sem(e(nc.semaphore("sgt")))
        self.sow = Sem(e(nc.semaphore("sow")))
        self.sg = Sem(e(nc.semaphore("sg")))
        self.sc = Sem(e(nc.semaphore("sc")))
        self.sv = Sem(e(nc.semaphore("sv")))
        self.sp = Sem(e(nc.semaphore("sp")))
        self.so = Sem(e(nc.semaphore("so")))

        self.phase0()

        esA = contextlib.ExitStack()
        self.xlnT = esA.enter_context(nc.sbuf_tensor("xlnT", [128, ND, S2], BF16))
        esKV = contextlib.ExitStack()
        self.k_sb = esKV.enter_context(nc.sbuf_tensor("k_sb", [128, NTO, D], BF16))
        self.v_sb = esKV.enter_context(nc.sbuf_tensor("v_sb", [128, NTO, D], BF16))
        with contextlib.ExitStack() as ps_es, nc.Block() as blk:
            self._mk_psum(ps_es)
            ps_a = [ps_es.enter_context(
                nc.psum_tensor(self._nm("ps_a") + str(i), [1, OCW], F32))
                for i in range(NQ)]
            for sp_i in (1, 0):
                self._proj_bias(blk, "k")
                w_done_k0 = self._get_wdma(blk, "k", 0)
                self.phase_ln1(blk, sp_i, after_tile=lambda t: self._proj_qt(
                    blk, "k", 0, t, w_done_k0, first_t=(t == 0)))
                self.free["wbuf0"] = (self.sp, self.sp.n)
                self.phase_proj(blk, sp_i, "k", start_q=1)
                tpq = NTO // NQ

                def after_q(q, sp_i=sp_i, blk=blk, tpq=tpq):
                    if q == NQ - 1:
                        return
                    for t in range(q * tpq, (q + 1) * tpq):
                        self._rope_norm_k_tile(blk, sp_i, t)
                self.phase_proj(blk, sp_i, "v", after_quarter=after_q)
                self.phase_pa(blk, sp_i, ps_a, late=range((NQ - 1) * tpq, NTO))
            self._emit_wdma(blk, "q", 0)
            self.phase_a_roundtrip(blk)
        esKV.close()

        esQ = contextlib.ExitStack()
        esX2 = contextlib.ExitStack()
        self.x2T = esX2.enter_context(
            nc.sbuf_tensor("x2T", [128, ND, S2], BF16, side="right"))
        self.q_sb = esQ.enter_context(
            nc.sbuf_tensor("q_sb", [128, NTO, D], BF16, side="right"))
        esQT = contextlib.ExitStack()
        with contextlib.ExitStack() as ps_es, nc.Block() as blk:
            self._mk_psum(ps_es)
            @blk.gpsimd
            def _(g):
                g.dma_start(
                    out=self.x2T[:, :, 0:OCW],
                    in_=self.w_oT.rearrange("(dc p) o -> p dc o", p=128)
                    [:, :, 0:OCW]).then_inc(self.sow.h, 16)
                self.sow.n += 16
            self.phase_proj(blk, 0, "q")
            esA.close()
            self.qT = esQT.enter_context(
                nc.sbuf_tensor("qT", [128, ND, S2], BF16))
            self.phase_rope_q_transpose(blk)
            for t in range(2):
                xs = self.x_scr[t]
                sxp = self.sxp[t]

                @blk.sync
                def _(s, t=t, xs=xs, sxp=sxp):
                    self._wait_free(s, f"x_scr{t}")
                    if t == 0:
                        s.wait_ge(self.sa.h, self.a_store_done)
                    s.dma_start(out=xs[:, :],
                                in_=self.x_all[t * 128:(t + 1) * 128, :]
                                ).then_inc(sxp.h, 16)
                    sxp.n += 16
                self.x1init_dma = getattr(self, "x1init_dma", {})
                self.x1init_dma[t] = sxp.n
        esQ.close()
        esX1 = contextlib.ExitStack()
        self.x1 = esX1.enter_context(
            nc.sbuf_tensor("x1", [128, NTO, D], F32, side="right"))
        with contextlib.ExitStack() as ps_es, nc.Block() as blk:
            self._mk_psum(ps_es, n_m=4)
            ps_l = ps_es.enter_context(
                nc.psum_tensor(self._nm("ps_l"), [128, E], F32))
            ps_gt = ps_es.enter_context(
                nc.psum_tensor(self._nm("ps_gt"), [self.NTO * 8, 128], BF16))
            self._proj_bias(blk, "o")
            self.phase_x1init(blk, tiles=[0, 1])
            self._emit_wdma(blk, "o", 1)
            self.phase_oproj(blk, ps_l)
            self.phase_gates(blk, ps_gt)
            self.phase_experts(blk)
            self.phase_biasmix(blk)
        self.phase_store()
        esX1.close()
        esQT.close()
        esX2.close()
        es0.close()
        return self.nc

    def _mk_psum(self, ps_es, tr=True, n_m=2):
        nc, OCW = self.nc, self.OCW
        if tr:
            self.ps_tr = [ps_es.enter_context(
                nc.psum_tensor(self._nm("ps_tr") + str(i), [128, 1024], BF16))
                for i in range(2)]
            for i in range(2):
                self.free.pop(f"tr{i}", None)
        self.ps_m = [ps_es.enter_context(
            nc.psum_tensor(self._nm("ps_m") + str(i), [128, OCW], F32))
            for i in range(n_m)]
        for i in range(4):
            self.free.pop(f"psm{i}", None)

    def phase0(self):
        nc = self.nc
        s0w, sg = self.s0w, self.sg
        with nc.Block() as blk:
            @blk.gpsimd
            def _(g):
                g.dma_start(out=self.gw_sb[:, :, :],
                            in_=self.g_wT.rearrange("(dc p) e -> p dc e", p=128)
                            ).then_inc(s0w.h, 16)
                s0w.n += 16
                g.dma_start(out=self.eb_sb[:, :], in_=self.e_b[:, :]
                            ).then_inc(s0w.h, 16)
                s0w.n += 16
                g.iota(self.iota_f[:, :], [[1, 128]], channel_multiplier=0,
                       allow_small_or_imprecise_dtypes=True)
                g.iota(self.iota_p[:, :1], [[1, 1]], channel_multiplier=1,
                       allow_small_or_imprecise_dtypes=True)
                sg.inc(g.memset(self.ones_c[:, :], 1.0))

            @blk.vector
            def _(v):
                v.wait_ge(sg.h, sg.n)
                v.tensor_scalar(out=self.ident_bf[:, :], in0=self.iota_f[:, :],
                                scalar1=self.iota_p[:, 0:1], scalar2=None,
                                op0=ALU.is_equal)

    def _ln_stats_tile(self, blk, stats, x_in):
        D = self.D
        sbuf, si = stats
        nch = D // 512 if D >= 512 else 1
        cw = D // nch

        @blk.vector
        def _(v):
            v.drain()
            for c in range(nch):
                v.bn_stats(out=sbuf[:, si, 6 * c:6 * c + 6],
                           in_=x_in[:, c * cw:(c + 1) * cw])
            v.drain()
            v.bn_aggr(out=sbuf[:, si, 24:26], in_=sbuf[:, si, 0:6 * nch])

    def _ln_chain(self, blk, sbuf, lo, n):
        sc, sv = self.sc, self.sv
        mean = sbuf[:, lo:lo + n, 24:25]
        var = sbuf[:, lo:lo + n, 25:26]
        vpe = sbuf[:, lo:lo + n, 26:27]
        recip = sbuf[:, lo:lo + n, 27:28]
        rstd = sbuf[:, lo:lo + n, 28:29]
        negmr = sbuf[:, lo:lo + n, 29:30]
        opr = sbuf[:, lo:lo + n, 30:31]

        @blk.vector
        def _(v):
            v.drain()
            v.tensor_scalar(out=vpe, in0=var, scalar1=LN_EPS, scalar2=None,
                            op0=ALU.add)
            v.drain()
            sv.inc(v.reciprocal(out=recip, in_=vpe))
        recip_done = sv.n

        @blk.scalar
        def _(s):
            s.wait_ge(sv.h, recip_done)
            sc.inc(s.activation(out=rstd, in_=recip, func=AF.Sqrt))
        sqrt_done = sc.n

        @blk.vector
        def _(v):
            v.wait_ge(sc.h, sqrt_done)
            v.scalar_tensor_tensor(out=negmr, in0=mean, scalar=-1.0, in1=rstd,
                                   op0=ALU.mult, op1=ALU.mult)
            sv.inc(v.tensor_scalar(out=opr, in0=rstd, scalar1=1.0,
                                   scalar2=None, op0=ALU.add))
        return sv.n

    def _ln_apply(self, blk, stats, x_in, out_bf, ready, xkey):
        sv = self.sv
        sbuf, si = stats

        @blk.vector
        def _(v):
            v.wait_ge(sv.h, ready)
            self._wait_free(v, xkey)
            v.drain()
            sv.inc(v.tensor_scalar(out=out_bf, in0=x_in,
                                   scalar1=sbuf[:, si, 28:29],
                                   scalar2=sbuf[:, si, 29:30],
                                   op0=ALU.mult, op1=ALU.add))
        return sv.n

    def _transpose_tile(self, blk, src_of, dstT, tblock, scalar_evac=True,
                        scale_of=None):
        ND = self.ND
        sp, sv = self.sp, self.sv
        nb = (ND + 7) // 8
        for b in range(nb):
            ps = self.ps_tr[b % 2]
            lo = b * 8
            hi = min(lo + 8, ND)

            @blk.tensor
            def _(t, ps=ps, lo=lo, hi=hi, b=b):
                self._wait_free(t, f"tr{b % 2}")
                for dc in range(lo, hi):
                    tr = t.transpose(ps[:, (dc - lo) * 128:(dc - lo + 1) * 128],
                                     src_of(dc), self.ident_bf[:, :])
                sp.inc(tr)
            tr_done = self.sp.n
            if scale_of is not None:
                if b % 2 == 0:
                    @blk.vector
                    def _(v, ps=ps, lo=lo, hi=hi, tr_done=tr_done, b=b):
                        v.wait_ge(sp.h, tr_done)
                        if tblock == 0 and b == 0:
                            v.wait_ge(self.sa.h, self.aT_load_done)
                        for dc in range(lo, hi):
                            ins = v.tensor_scalar(
                                out=dstT[:, dc,
                                         tblock * 128:(tblock + 1) * 128],
                                in0=ps[:, (dc - lo) * 128:(dc - lo + 1) * 128],
                                scalar1=scale_of(dc), scalar2=None,
                                op0=ALU.mult)
                        sv.inc(ins)
                    self.free[f"tr{b % 2}"] = (sv, sv.n)
                else:
                    @blk.scalar
                    def _(s, ps=ps, lo=lo, hi=hi, tr_done=tr_done):
                        s.wait_ge(sp.h, tr_done)
                        if tblock == 0:
                            s.wait_ge(self.sa.h, self.aT_load_done)
                        for dc in range(lo, hi):
                            ins = s.activation(
                                out=dstT[:, dc,
                                         tblock * 128:(tblock + 1) * 128],
                                in_=ps[:, (dc - lo) * 128:(dc - lo + 1) * 128],
                                func=AF.Identity, scale=scale_of(dc))
                        self.sc.inc(ins)
                    self.free[f"tr{b % 2}"] = (self.sc, self.sc.n)
            elif b % 2 == 0 or not scalar_evac:
                @blk.vector
                def _(v, ps=ps, lo=lo, hi=hi, tr_done=tr_done):
                    v.wait_ge(sp.h, tr_done)
                    sv.inc(v.tensor_copy(
                        out=dstT[:, lo:hi, tblock * 128:(tblock + 1) * 128],
                        in_=ps[:, 0:(hi - lo) * 128]))
                self.free[f"tr{b % 2}"] = (sv, sv.n)
            else:
                @blk.scalar
                def _(s, ps=ps, lo=lo, hi=hi, tr_done=tr_done):
                    s.wait_ge(sp.h, tr_done)
                    self.sc.inc(s.activation(
                        out=dstT[:, lo:hi, tblock * 128:(tblock + 1) * 128],
                        in_=ps[:, 0:(hi - lo) * 128], func=AF.Copy))
                self.free[f"tr{b % 2}"] = (self.sc, self.sc.n)
        self.free["xln_scr_last_tr"] = (sp, self.sp.n)
        self.tile_ready[tblock] = (self.sv.n, self.sc.n)

    def phase_ln1(self, blk, sp_i, after_tile=None):
        NTO = self.NTO
        sc, sv = self.sc, self.sv
        self.tile_ready = {}
        BL = 2
        for t0 in range(0, NTO, BL):
            for t in range(t0, t0 + BL):
                g = sp_i * NTO + t
                xs = self.x_scr[t % 2]
                sxp = self.sxp[t % 2]

                @blk.sync
                def _(s, t=t, g=g, xs=xs, sxp=sxp):
                    self._wait_free(s, f"x_scr{t % 2}")
                    s.dma_start(out=xs[:, :],
                                in_=self.x_all[g * 128:(g + 1) * 128, :]
                                ).then_inc(sxp.h, 16)
                    sxp.n += 16
                dma_done = sxp.n

                @blk.vector
                def _(v, dma_done=dma_done, sxp=sxp):
                    v.wait_ge(sxp.h, dma_done)
                self._ln_stats_tile(blk, (self.st1, g), xs[:, :])
            ready = self._ln_chain(blk, self.st1, sp_i * NTO + t0, BL)
            for t in range(t0, t0 + BL):
                g = sp_i * NTO + t
                xs = self.x_scr[t % 2]
                xln = self.xln_scr[t % 2]
                apply_done = self._ln_apply(blk, (self.st1, g), xs[:, :],
                                            xln[:, :], ready, f"xlnf{t % 2}")
                self.free[f"x_scr{t % 2}"] = (sv, apply_done)

                @blk.tensor
                def _(te, apply_done=apply_done):
                    te.wait_ge(sv.h, apply_done)
                self._transpose_tile(
                    blk, lambda dc, xln=xln: xln[:, dc * 128:(dc + 1) * 128],
                    self.xlnT, t)
                self.free[f"xlnf{t % 2}"] = (self.sp, self.sp.n)
            if after_tile is not None:
                for t in range(t0, t0 + BL):
                    after_tile(t)

    def _emit_wdma(self, blk, which, q):
        OCW = self.OCW
        wT = {"q": self.w_qT, "k": self.w_kT, "v": self.w_vT,
              "o": self.w_oT}[which]
        swp = self.swp[q % 2]

        @blk.gpsimd
        def _(g, q=q, swp=swp, wT=wT):
            self._wait_free(g, f"wbuf{q % 2}")
            g.dma_start(
                out=self.wbuf[q % 2][:, :, :],
                in_=wT.rearrange("(dc p) o -> p dc o", p=128)
                [:, :, q * OCW:(q + 1) * OCW]).then_inc(swp.h, 16)
            swp.n += 16
        self.wdma_done[(which, q)] = swp.n

    def _get_wdma(self, blk, which, q):
        if (which, q) not in self.wdma_done:
            self._emit_wdma(blk, which, q)
        return self.wdma_done.pop((which, q))

    def _proj_bias(self, blk, which):
        brow = {"q": 0, "k": 1, "v": 2, "o": 3}[which]
        sbb = self.sbb

        @blk.gpsimd
        def _(g, sbb=sbb, brow=brow):
            self._wait_free(g, "bb_scr")
            g.dma_start(out=self.bb_scr[:, :],
                        in_=self.biases[brow:brow + 1, :]
                        .partition_broadcast(128)).then_inc(sbb.h, 16)
            sbb.n += 16
        self.bias_done = sbb.n

    def _proj_qt(self, blk, which, q, t, w_done, first_t):
        NTO, OCW, ND = self.NTO, self.OCW, self.ND
        sc, sv, sp = self.sc, self.sv, self.sp
        dst = {"k": getattr(self, "k_sb", None),
               "v": getattr(self, "v_sb", None),
               "q": getattr(self, "q_sb", None)}[which]
        bias_done = self.bias_done
        self.psm_ctr += 1
        nm_ = len(self.ps_m)
        ps = self.ps_m[self.psm_ctr % nm_]
        pnm = f"psm{self.psm_ctr % nm_}"
        tile_rdy = self.tile_ready.get(t)

        @blk.tensor
        def _(te, q=q, t=t, ps=ps, w_done=w_done, pnm=pnm):
            te.wait_ge(self.swp[q % 2].h, w_done)
            if q == 0 and tile_rdy is not None:
                te.wait_ge(sv.h, tile_rdy[0])
                te.wait_ge(self.sc.h, tile_rdy[1])
            self._wait_free(te, pnm)
            for dc in range(ND):
                mm = te.matmul(
                    ps[:, :],
                    self.xlnT[:, dc, t * 128:(t + 1) * 128],
                    self.wbuf[q % 2][:, dc, :],
                    start=(dc == 0), stop=(dc == ND - 1))
            sp.inc(mm)
        mmc = sp.n

        @blk.vector
        def _(v, q=q, t=t, ps=ps, mmc=mmc, bias_done=bias_done):
            v.wait_ge(sp.h, mmc)
            v.wait_ge(self.sbb.h, bias_done)
            sv.inc(v.tensor_add(
                out=dst[:, t, q * OCW:(q + 1) * OCW],
                in0=ps[:, :],
                in1=self.bb_scr[:, q * OCW:(q + 1) * OCW]))
        self.free[pnm] = (sv, sv.n)
        self.kv_evac[(which, t)] = sv.n

    def phase_proj(self, blk, sp_i, which, after_quarter=None, start_q=0):
        NTO, NQ = self.NTO, self.NQ
        sp = self.sp
        if start_q == 0:
            self._proj_bias(blk, which)
        for q in range(start_q, NQ):
            w_done = self._get_wdma(blk, which, q)
            for t in range(NTO):
                self._proj_qt(blk, which, q, t, w_done, first_t=(t == 0))
            self.free[f"wbuf{q % 2}"] = (sp, sp.n)
            if after_quarter is not None:
                after_quarter(q)
        self.free["bb_scr"] = (self.sv, self.sv.n)

    def _rope_tile(self, blk, buf, t, g, which):
        RH = self.RH
        sv = self.sv
        sct = self.sct[t % 2]
        ct = self.ct_scr[t % 2]
        sn = self.sn_scr[t % 2]

        @blk.sync
        def _(s, t=t, g=g, sct=sct):
            self._wait_free(s, f"ct{t % 2}")
            s.dma_start(out=ct[:, :],
                        in_=self.cos_t[g * 128:(g + 1) * 128, :]
                        ).then_inc(sct.h, 16)
            s.dma_start(out=sn[:, :],
                        in_=self.sin_t[g * 128:(g + 1) * 128, :]
                        ).then_inc(sct.h, 16)
            sct.n += 32
        tab_done = sct.n

        @blk.vector
        def _(v, t=t, tab_done=tab_done, sct=sct):
            v.wait_ge(sct.h, tab_done)
            self._wait_free(v, "tmp8")
            v.drain()
            ev = buf[:, t, 0:RH]
            od = buf[:, t, RH:2 * RH]
            rt = [self.tmp8[:, i, :] for i in range(4)]
            v.tensor_mul(out=rt[0], in0=ev, in1=ct[:, :])
            v.tensor_mul(out=rt[1], in0=od, in1=sn[:, :])
            v.tensor_mul(out=rt[2], in0=od, in1=ct[:, :])
            v.tensor_mul(out=rt[3], in0=ev, in1=sn[:, :])
            v.drain()
            v.tensor_sub(out=ev, in0=rt[0], in1=rt[1])
            sv.inc(v.tensor_add(out=od, in0=rt[2], in1=rt[3]))
        self.free[f"ct{t % 2}"] = (sv, sv.n)
        self.free["tmp8"] = (sv, sv.n)

    def _rope_norm_k_tile(self, blk, sp_i, t):
        sv = self.sv
        g = sp_i * self.NTO + t
        self._rope_tile(blk, self.k_sb, t, g, "k")

        @blk.vector
        def _(v, t=t, g=g):
            self._wait_free(v, "tmp8")
            v.drain()
            sv.inc(v.scalar_tensor_tensor(
                out=self.tmp8.rearrange("p a b -> p (a b)"),
                in0=self.k_sb[:, t, :], scalar=1.0,
                in1=self.k_sb[:, t, :], op0=ALU.mult, op1=ALU.mult,
                accum_out=self.kst[:, g, 0:1]))
        self.free["tmp8"] = (sv, sv.n)
        self.ssq_done = getattr(self, "ssq_done", {})
        self.ssq_done[g] = sv.n

    def _rnorm_batch(self, blk, g0, lo, hi):
        sv = self.sv
        ssq = self.kst[:, g0 + lo:g0 + hi, 0:1]
        nrm = self.kst[:, g0 + lo:g0 + hi, 1:2]
        clm = self.kst[:, g0 + lo:g0 + hi, 2:3]
        rnm = self.kst[:, g0 + lo:g0 + hi, 3:4]
        ssq_all = self.ssq_done[g0 + hi - 1]

        @blk.scalar
        def _(s, ssq_all=ssq_all, ssq=ssq, nrm=nrm):
            s.wait_ge(sv.h, ssq_all)
            self.sc.inc(s.activation(out=nrm, in_=ssq, func=AF.Sqrt))
        nrm_done = self.sc.n

        @blk.vector
        def _(v, nrm_done=nrm_done, nrm=nrm, clm=clm, rnm=rnm, lo=lo, hi=hi):
            v.wait_ge(self.sc.h, nrm_done)
            v.tensor_scalar(out=clm, in0=nrm, scalar1=NORM_EPS, scalar2=None,
                            op0=ALU.max)
            v.drain()
            v.reciprocal(out=rnm, in_=clm)
            v.drain()
            sv.inc(v.tensor_copy(out=self.rn_bf[:, lo:hi], in_=rnm))
        return sv.n

    def phase_pa(self, blk, sp_i, ps_a, late=()):
        NTO, NQ, OCW = self.NTO, self.NQ, self.OCW
        sv, sp = self.sv, self.sp
        first = sp_i == 1
        last = sp_i == 0
        late = set(late)
        order = [t for t in range(NTO) if t not in late] + sorted(late)
        g0 = sp_i * NTO
        n_early = NTO - len(late)
        rn_early = self._rnorm_batch(blk, g0, 0, n_early) if n_early else None
        rn_of = lambda t: rn_early if t < n_early else rn_late

        def pa_tile(t):
            is_late = t in late
            @blk.vector
            def _(v, t=t, is_late=is_late):
                if t == order[0] or is_late:
                    v.drain()
                sv.inc(v.tensor_mul(out=self.v_sb[:, t, :],
                                    in0=self.k_sb[:, t, :],
                                    in1=self.v_sb[:, t, :]))
            p_done = sv.n

            is_first = first and t == order[0]
            is_last = last and t == order[-1]

            @blk.tensor
            def _(te, t=t, p_done=p_done, is_first=is_first, is_last=is_last,
                  rn_done=rn_of(t)):
                te.wait_ge(sv.h, p_done)
                te.wait_ge(sv.h, rn_done)
                for qi in range(NQ):
                    mm = te.matmul(ps_a[qi][:, :], self.rn_bf[:, t:t + 1],
                                   self.v_sb[:, t, qi * OCW:(qi + 1) * OCW],
                                   start=is_first, stop=is_last)
                if is_last:
                    sp.inc(mm)

        for t in order[:n_early]:
            pa_tile(t)
        for t in sorted(late):
            self._rope_norm_k_tile(blk, sp_i, t)
        rn_late = self._rnorm_batch(blk, g0, n_early, NTO) if late else None
        for t in order[n_early:]:
            pa_tile(t)
        if last:
            a_done = self.sp.n

            @blk.vector
            def _(v):
                v.wait_ge(sp.h, a_done)
                self._wait_free(v, "x_scr0")
                v.drain()
                for qi in range(NQ):
                    ins = v.tensor_copy(
                        out=self.x_scr[0][0:1, qi * OCW:(qi + 1) * OCW],
                        in_=ps_a[qi][:, :])
                self.sv.inc(ins)
            self.free["x_scr0"] = (self.sv, self.sv.n)
            self.a_row_ready = self.sv.n

    def phase_a_roundtrip(self, blk):
        sa, sv = self.sa, self.sv
        a_done = self.a_row_ready

        @blk.sync
        def _(s):
            s.wait_ge(sv.h, a_done)
            s.dma_start(out=self.a_hbm[None, :], in_=self.x_scr[0][0:1, :]
                        ).then_inc(sa.h, 16)
            sa.n += 16
            s.wait_ge(sa.h, sa.n)
            self.a_store_done = sa.n
            s.dma_start(out=self.aT[:, :],
                        in_=self.a_hbm.rearrange("(dc p) -> p dc", p=128),
                        allow_slow_non_contiguous=True,
                        ).then_inc(sa.h, 16)
            sa.n += 16
        self.aT_load_done = sa.n

    def phase_rope_q_transpose(self, blk):
        ND, NTO = self.ND, self.NTO
        sa, sv = self.sa, self.sv
        self.tile_ready = {}
        for t in range(NTO):
            self._rope_tile(blk, self.q_sb, t, t, "q")
            rope_done = sv.n

            @blk.tensor
            def _(te, rope_done=rope_done):
                te.wait_ge(sv.h, rope_done)
            self._transpose_tile(
                blk, lambda dc, t=t: self.q_sb[:, t, dc * 128:(dc + 1) * 128],
                self.qT, t,
                scale_of=lambda dc: self.aT[:, dc:dc + 1])

    def _scale_wquarter(self, blk, which, q):
        ND = self.ND
        sv, sa = self.sv, self.sa
        swp = self.swp[q % 2]
        w_done = self._get_wdma(blk, which, q)

        @blk.vector
        def _(v, q=q, w_done=w_done, swp=swp):
            v.wait_ge(swp.h, w_done)
            v.wait_ge(sa.h, self.aT_load_done)
            for dc in range(ND):
                ins = v.tensor_scalar(out=self.wbuf[q % 2][:, dc, :],
                                      in0=self.wbuf[q % 2][:, dc, :],
                                      scalar1=self.aT[:, dc:dc + 1],
                                      scalar2=None, op0=ALU.mult)
            sv.inc(ins)
        self.wscale_done = getattr(self, "wscale_done", {})
        self.wscale_done[q] = (sv.n, w_done)

    def phase_x1init(self, blk, tiles=None):
        NTO = self.NTO
        sv = self.sv
        if tiles is None:
            self._proj_bias(blk, "o")
            tiles = range(NTO)
        for t in tiles:
            xs = self.x_scr[t % 2]
            sxp = self.sxp[t % 2]
            if t < 2:
                dma_done = self.x1init_dma[t]
            else:
                @blk.sync
                def _(s, t=t, xs=xs, sxp=sxp):
                    self._wait_free(s, f"x_scr{t % 2}")
                    s.dma_start(out=xs[:, :],
                                in_=self.x_all[t * 128:(t + 1) * 128, :]
                                ).then_inc(sxp.h, 16)
                    sxp.n += 16
                dma_done = sxp.n

            @blk.vector
            def _(v, t=t, xs=xs, dma_done=dma_done, sxp=sxp):
                v.wait_ge(sxp.h, dma_done)
                if t == 0:
                    v.wait_ge(self.sbb.h, self.bias_done)
                v.drain()
                v.tensor_scalar(out=self.x1[:, t, :], in0=xs[:, :],
                                scalar1=self.st1[:, t, 30:31],
                                scalar2=self.st1[:, t, 29:30],
                                op0=ALU.mult, op1=ALU.add)
                v.drain()
                sv.inc(v.tensor_add(out=self.x1[:, t, :],
                                    in0=self.x1[:, t, :],
                                    in1=self.bb_scr[:, :]))
            self.free[f"x_scr{t % 2}"] = (sv, sv.n)

    def phase_oproj(self, blk, ps_l):
        NTO, NQ, OCW, ND = self.NTO, self.NQ, self.OCW, self.ND
        sv, sp = self.sv, self.sp
        self.tile_ready = {}
        for q in range(NQ):
            if q == 0:
                swp, w_done = self.sow, 16
                rhs_of = lambda dc: self.x2T[:, dc, 0:OCW]
            else:
                swp = self.swp[q % 2]
                w_done = self._get_wdma(blk, "o", q)
                rhs_of = lambda dc, q=q: self.wbuf[q % 2][:, dc, :]
            for t in range(NTO):
                nm_ = len(self.ps_m)
                ps = self.ps_m[(q * NTO + t) % nm_]
                pnm = f"psm{(q * NTO + t) % nm_}"

                @blk.tensor
                def _(te, q=q, t=t, ps=ps, w_done=w_done, swp=swp, pnm=pnm,
                      rhs_of=rhs_of):
                    te.wait_ge(swp.h, w_done)
                    self._wait_free(te, pnm)
                    for dc in range(ND):
                        mm = te.matmul(
                            ps[:, :],
                            self.qT[:, dc, t * 128:(t + 1) * 128],
                            rhs_of(dc),
                            start=(dc == 0), stop=(dc == ND - 1))
                    sp.inc(mm)
                mmc = sp.n

                @blk.vector
                def _(v, q=q, t=t, ps=ps, mmc=mmc):
                    v.wait_ge(sp.h, mmc)
                    if q == 0:
                        v.drain()
                    sv.inc(v.scalar_tensor_tensor(
                        out=self.x1[:, t, q * OCW:(q + 1) * OCW],
                        in0=ps[:, :], scalar=1.0,
                        in1=self.x1[:, t, q * OCW:(q + 1) * OCW],
                        op0=ALU.mult, op1=ALU.add))
                self.free[pnm] = (sv, sv.n)
                if q == 0 and t + 2 < NTO:
                    self.phase_x1init(blk, tiles=[t + 2])
                if q == NQ - 1:
                    self._ln2_tile(blk, ps_l, t)
            if q != 0:
                self.free[f"wbuf{q % 2}"] = (sp, sp.n)
        self.free["bb_scr"] = (sv, self.sv.n)

    def _ln2_tile(self, blk, ps_l, t):
        self._ln_stats_tile(blk, (self.st2, t), self.x1[:, t, :])
        self._ln2_pend.append(t)
        if len(self._ln2_pend) == 4 or t == self.NTO - 1:
            tiles = self._ln2_pend
            self._ln2_pend = []
            ready = self._ln_chain(blk, self.st2, tiles[0], len(tiles))
            for tt in tiles:
                self._ln2_tail(blk, ps_l, tt, ready)

    def _ln2_tail(self, blk, ps_l, t, ready):
        ND = self.ND
        sc, sv, sp, s0w = self.sc, self.sv, self.sp, self.s0w
        xln = self.xln_scr[t % 2]
        apply_done = self._ln_apply(blk, (self.st2, t), self.x1[:, t, :],
                                    xln[:, :], ready, f"xlnf{t % 2}")

        @blk.tensor
        def _(te, apply_done=apply_done):
            te.wait_ge(sv.h, apply_done)
        self._transpose_tile(
            blk, lambda dc, xln=xln: xln[:, dc * 128:(dc + 1) * 128],
            self.x2T, t, scalar_evac=False)
        self.free[f"xlnf{t % 2}"] = (self.sp, self.sp.n)
        x2T_done = self.tile_ready[t]

        @blk.tensor
        def _(te, t=t, x2T_done=x2T_done):
            te.wait_ge(s0w.h, 32)
            self._wait_free(te, "ps_l")
            te.wait_ge(sv.h, x2T_done[0])
            te.wait_ge(sc.h, x2T_done[1])
            for dc in range(ND):
                mm = te.matmul(ps_l[:, :],
                               self.x2T[:, dc, t * 128:(t + 1) * 128],
                               self.gw_sb[:, dc, :],
                               start=(dc == 0), stop=(dc == ND - 1))
            self.sp.inc(mm)
        lg_done = self.sp.n

        @blk.vector
        def _(v, t=t, lg_done=lg_done):
            v.wait_ge(sp.h, lg_done)
            sv.inc(v.tensor_copy(out=self.logit[:, t, :], in_=ps_l[:, :]))
        self.free["ps_l"] = (sv, sv.n)

    def phase_gates(self, blk, ps_gt):
        NTO, E = self.NTO, self.E
        sc, sv, sp = self.sc, self.sv, self.sp
        L = self.logit[:, :, :]
        m1 = self.gst[:, :, 0:1]
        m2 = self.gst[:, :, 1:2]
        w2 = self.gst[:, :, 2:3]
        w1 = self.gst[:, :, 3:4]

        def bc(ap):
            return ap.broadcast_to([128, NTO, E])

        @blk.vector
        def _(v):
            v.drain()
            v.tensor_reduce(out=m1, in_=L, axis=AXX, op=ALU.max)
            v.drain()
            v.tensor_tensor(out=self.eq1s[:, :, :], in0=L, in1=bc(m1),
                            op=ALU.is_equal)
            v.drain()
            v.scalar_tensor_tensor(out=self.l2s[:, :, :],
                                   in0=self.eq1s[:, :, :],
                                   scalar=-BIG_NEG, in1=L,
                                   op0=ALU.mult, op1=ALU.add)
            v.drain()
            v.tensor_reduce(out=m2, in_=self.l2s[:, :, :], axis=AXX,
                            op=ALU.max)
            v.drain()
            sv.inc(v.tensor_tensor(out=w2, in0=m2, in1=m1, op=ALU.subtract))
        d_done = sv.n

        @blk.scalar
        def _(s, d_done=d_done):
            s.wait_ge(sv.h, d_done)
            sc.inc(s.activation(out=w2, in_=w2, func=AF.Sigmoid))
        sig_done = sc.n

        @blk.vector
        def _(v, sig_done=sig_done):
            v.wait_ge(sc.h, sig_done)
            v.tensor_scalar(out=w1, in0=w2, scalar1=-1.0, scalar2=1.0,
                            op0=ALU.mult, op1=ALU.add)
            v.drain()
            v.tensor_tensor(out=self.l2s[:, :, :], in0=self.l2s[:, :, :],
                            in1=bc(m2), op=ALU.is_equal)
            v.drain()
            v.tensor_tensor(out=self.l2s[:, :, :], in0=self.l2s[:, :, :],
                            in1=bc(w2), op=ALU.mult)
            v.tensor_tensor(out=self.eq1s[:, :, :], in0=self.eq1s[:, :, :],
                            in1=bc(w1), op=ALU.mult)
            v.drain()
            v.tensor_tensor(out=self.gates[:, :, :], in0=self.eq1s[:, :, :],
                            in1=self.l2s[:, :, :], op=ALU.add)
            v.drain()
            sv.inc(v.tensor_copy(out=self.gb_scr[:, :, :],
                                 in_=self.gates[:, :, :]))
        g_done = sv.n

        @blk.tensor
        def _(te, g_done=g_done):
            te.wait_ge(sv.h, g_done)
            self.sp.inc(te.transpose(
                ps_gt[:, :],
                self.gb_scr.rearrange("p t e -> p (t e)"),
                self.ident_bf[:, :]))
        gt_done = self.sp.n

        gpk = self.tmp8.rearrange("p a b -> p (a b)")[0:NTO * E, 0:128]

        @blk.vector
        def _(v, gt_done=gt_done):
            v.wait_ge(sp.h, gt_done)
            self._wait_free(v, "tmp8")
            sv.inc(v.tensor_copy(out=gpk, in_=ps_gt[:, :]))
        self.free["ps_gt_ev"] = (sv, sv.n)
        pk_done = sv.n

        @blk.sync
        def _(s, pk_done=pk_done, gpk=gpk):
            s.wait_ge(sv.h, pk_done)
            s.dma_start(out=self.gt_hbm[:, :], in_=gpk
                        ).then_inc(self.sgt.h, 16)
            self.sgt.n += 16
            s.wait_ge(self.sgt.h, self.sgt.n)
            s.dma_start(out=self.gTb[:, :, :],
                        in_=self.gt_hbm.rearrange("(t e) n -> e t n", e=8)
                        ).then_inc(self.sgt.h, 16)
            self.sgt.n += 16
        self.free["ps_gt"] = (self.sgt, self.sgt.n)

    def phase_biasmix(self, blk):
        NTO, NQ, OCW = self.NTO, self.NQ, self.OCW
        sv, sp, s0w = self.sv, self.sp, self.s0w
        so = self.so
        gTb_sem, gTb_done = self.free["ps_gt"]
        for t in range(NTO):
            for q in range(NQ):
                nm_ = len(self.ps_m)
                ps = self.ps_m[(t * NQ + q) % nm_]
                pnm = f"psm{(t * NQ + q) % nm_}"

                @blk.tensor
                def _(te, t=t, q=q, ps=ps, pnm=pnm):
                    te.wait_ge(s0w.h, 32)
                    if (t, q) == (0, 0):
                        te.wait_ge(gTb_sem.h, gTb_done)
                    self._wait_free(te, pnm)
                    self.sp.inc(te.matmul(
                        ps[:, :], self.gTb[:, t, :],
                        self.eb_sb[:, q * OCW:(q + 1) * OCW],
                        start=True, stop=True))
                mm_done = self.sp.n

                @blk.vector
                def _(v, t=t, q=q, ps=ps, mm_done=mm_done):
                    v.wait_ge(sp.h, mm_done)
                    if (t, q) == (0, 0):
                        v.drain()
                    sv.inc(v.scalar_tensor_tensor(
                        out=self.x1[:, t, q * OCW:(q + 1) * OCW],
                        in0=ps[:, :], scalar=1.0,
                        in1=self.x1[:, t, q * OCW:(q + 1) * OCW],
                        op0=ALU.mult, op1=ALU.add))
                self.free[pnm] = (sv, sv.n)
            tile_done = sv.n

            @blk.sync
            def _(s, t=t, tile_done=tile_done):
                s.wait_ge(sv.h, tile_done)
                s.dma_start(out=self.y[t * 128:(t + 1) * 128, :],
                            in_=self.x1[:, t, :]).then_inc(so.h, 16)
                so.n += 16

    def phase_experts(self, blk):
        ND, NTO, NQ, OCW, E = self.ND, self.NTO, self.NQ, self.OCW, self.E
        sv, sp = self.sv, self.sp
        steps = [(e_i, q) for e_i in range(E) for q in range(NQ)]
        for si, (e_i, q) in enumerate(steps):
            swp = self.swp[si % 2]

            @blk.gpsimd
            def _(g, si=si, e_i=e_i, q=q, swp=swp):
                self._wait_free(g, f"wbuf{si % 2}")
                g.dma_start(
                    out=self.wbuf[si % 2][:, :, :],
                    in_=self.e_w.rearrange("e (dc p) o -> e p dc o", p=128)
                    [e_i, :, :, q * OCW:(q + 1) * OCW]).then_inc(swp.h, 16)
                swp.n += 16
            w_done = swp.n
            for t in range(NTO):
                nm_ = len(self.ps_m)
                ps = self.ps_m[(si * NTO + t) % nm_]
                pnm = f"psm{(si * NTO + t) % nm_}"

                x2rdy = self.tile_ready[t]

                @blk.tensor
                def _(te, si=si, t=t, ps=ps, w_done=w_done, swp=swp, pnm=pnm,
                      x2rdy=x2rdy):
                    te.wait_ge(swp.h, w_done)
                    if si == 0:
                        te.wait_ge(sv.h, x2rdy[0])
                        te.wait_ge(self.sc.h, x2rdy[1])
                    self._wait_free(te, pnm)
                    for dc in range(ND):
                        mm = te.matmul(
                            ps[:, :],
                            self.x2T[:, dc, t * 128:(t + 1) * 128],
                            self.wbuf[si % 2][:, dc, :],
                            start=(dc == 0), stop=(dc == ND - 1))
                    self.sp.inc(mm)
                mmc = self.sp.n

                @blk.vector
                def _(v, e_i=e_i, q=q, t=t, ps=ps, mmc=mmc):
                    v.wait_ge(sp.h, mmc)
                    if t == 0:
                        v.drain()
                    sv.inc(v.scalar_tensor_tensor(
                        out=self.x1[:, t, q * OCW:(q + 1) * OCW],
                        in0=ps[:, :], scalar=self.gates[:, t, e_i:e_i + 1],
                        in1=self.x1[:, t, q * OCW:(q + 1) * OCW],
                        op0=ALU.mult, op1=ALU.add))
                self.free[pnm] = (sv, sv.n)
            self.free[f"wbuf{si % 2}"] = (sp, self.sp.n)

    def phase_store(self):
        so = self.so
        with self.nc.Block() as blk:
            @blk.sync
            def _(s):
                s.wait_ge(so.h, so.n)


def _host_prep(L0_attn_W_in, L0_attn_b_in, L0_attn_W_out, L0_attn_b_out,
               L0_ffn_gate_w, L0_ffn_expert_w, L0_ffn_expert_b, S, D):
    rot = D // 2
    perm = np.concatenate([np.arange(0, rot, 2), np.arange(1, rot, 2),
                           np.arange(rot, D)])
    w_q = L0_attn_W_in[0:D][perm]
    w_k = L0_attn_W_in[D:2 * D][perm]
    w_v = L0_attn_W_in[2 * D:3 * D][perm]
    w_qT = np.ascontiguousarray(w_q.T)
    w_kT = np.ascontiguousarray(w_k.T)
    w_vT = np.ascontiguousarray(w_v.T)
    w_oT = np.ascontiguousarray(L0_attn_W_out[:, perm].T)
    biases = np.ascontiguousarray(np.stack([
        L0_attn_b_in[0:D][perm], L0_attn_b_in[D:2 * D][perm],
        L0_attn_b_in[2 * D:3 * D][perm], L0_attn_b_out]).astype(np.float32))
    g_wT = np.ascontiguousarray(L0_ffn_gate_w.T)
    e_w = np.ascontiguousarray(L0_ffn_expert_w)
    e_b = np.ascontiguousarray(L0_ffn_expert_b)
    inv = 1.0 / (10000.0 ** (np.arange(0, rot, 2, dtype=np.float64) / rot))
    ang = np.arange(S, dtype=np.float64)[:, None] * inv[None, :]
    cos_t = np.cos(ang).astype(ml_dtypes.bfloat16)
    sin_t = np.sin(ang).astype(ml_dtypes.bfloat16)
    return dict(w_qT=w_qT, w_kT=w_kT, w_vT=w_vT, w_oT=w_oT, biases=biases,
                g_wT=g_wT, e_w=e_w, e_b=e_b, cos_t=cos_t, sin_t=sin_t)


def _in_maps(X, shared, S, n_cores):
    S2 = S // 2
    maps = []
    for c in range(n_cores):
        b, h = c // 2, c % 2
        xb = np.ascontiguousarray(X[b])
        if h == 0:
            x_all, cos_c, sin_c = xb, shared["cos_t"], shared["sin_t"]
        else:
            x_all = np.ascontiguousarray(
                np.concatenate([xb[S2:], xb[:S2]], axis=0))
            cos_c = np.ascontiguousarray(np.concatenate(
                [shared["cos_t"][S2:], shared["cos_t"][:S2]], axis=0))
            sin_c = np.ascontiguousarray(np.concatenate(
                [shared["sin_t"][S2:], shared["sin_t"][:S2]], axis=0))
        m = dict(shared)
        m["x_all"] = x_all
        m["cos_t"] = cos_c
        m["sin_t"] = sin_c
        maps.append(m)
    return maps


def kernel(L0_attn_X, L0_attn_W_in, L0_attn_b_in, L0_attn_W_out, L0_attn_b_out,
           L0_ffn_gate_w, L0_ffn_expert_w, L0_ffn_expert_b):
    B, S, D = L0_attn_X.shape
    E = L0_ffn_gate_w.shape[0]
    shared = _host_prep(np.asarray(L0_attn_W_in, dtype=np.float32),
                        np.asarray(L0_attn_b_in, dtype=np.float32),
                        np.asarray(L0_attn_W_out, dtype=np.float32),
                        np.asarray(L0_attn_b_out, dtype=np.float32),
                        np.asarray(L0_ffn_gate_w, dtype=np.float32),
                        np.asarray(L0_ffn_expert_w, dtype=np.float32),
                        np.asarray(L0_ffn_expert_b, dtype=np.float32), S, D)
    X = np.asarray(L0_attn_X, dtype=np.float32)
    maps = _in_maps(X, shared, S, N_CORES)
    nc = Builder(S, D, E, 512).build()
    res = run_bass_kernel_spmd(nc, maps, core_ids=list(range(N_CORES)))
    S2 = S // 2
    out = np.empty((B, S, D), dtype=np.float32)
    for c in range(N_CORES):
        b, h = c // 2, c % 2
        out[b, h * S2:(h + 1) * S2] = res.results[c]["y"]
    return out


# version 33 (final)
# speedup vs baseline: 1.0295x; 1.0216x over previous
import contextlib
import numpy as np
import ml_dtypes
import concourse.bass as bass
import concourse.mybir as mybir
from concourse.bass_utils import run_bass_kernel_spmd

F32 = mybir.dt.float32
BF16 = mybir.dt.bfloat16
AF = mybir.ActivationFunctionType
ALU = mybir.AluOpType
AXX = mybir.AxisListType.X

N_CORES = 8
BIG_NEG = 1.0e9
LN_EPS = 1.0e-5
NORM_EPS = 1.0e-12


class Sem:
    __slots__ = ("h", "n")

    def __init__(self, h):
        self.h = h
        self.n = 0

    def inc(self, inst, k=1):
        inst.then_inc(self.h, k)
        self.n += k
        return self.n


class Builder:

    def __init__(self, S, D, E, OCW):
        assert D % 128 == 0 and S % 256 == 0 and D % OCW == 0
        self.S, self.D, self.E, self.OCW = S, D, E, OCW
        self.ND = D // 128
        self.NT = S // 128
        self.NTO = self.NT // 2
        self.NQ = D // OCW
        self.RH = D // 4
        self.nc = bass.Bass("TRN2", target_bir_lowering=False)
        self._uid = 0
        self.free = {}
        self.wdma_done = {}
        self.kv_evac = {}
        self.bias_stage_done = {}
        self.psm_ctr = 0
        self._ln2_pend = []
        self._pq_pend = []

    def _nm(self, base):
        self._uid += 1
        return f"{base}_{self._uid}"

    def _wait_free(self, eng, name):
        fr = self.free.get(name)
        if fr is not None:
            eng.wait_ge(fr[0].h, fr[1])

    def build(self):
        nc = self.nc
        S, D, E, OCW = self.S, self.D, self.E, self.OCW
        ND, NT, NTO, NQ, RH = self.ND, self.NT, self.NTO, self.NQ, self.RH
        S2 = S // 2

        self.x_all = nc.dram_tensor("x_all", [S, D], F32, kind="ExternalInput").ap()
        self.cos_t = nc.dram_tensor("cos_t", [S, RH], BF16, kind="ExternalInput").ap()
        self.sin_t = nc.dram_tensor("sin_t", [S, RH], BF16, kind="ExternalInput").ap()
        self.w_qT = nc.dram_tensor("w_qT", [D, D], F32, kind="ExternalInput").ap()
        self.w_kT = nc.dram_tensor("w_kT", [D, D], F32, kind="ExternalInput").ap()
        self.w_vT = nc.dram_tensor("w_vT", [D, D], F32, kind="ExternalInput").ap()
        self.w_oT = nc.dram_tensor("w_oT", [D, D], F32, kind="ExternalInput").ap()
        self.biases = nc.dram_tensor("biases", [4, D], F32, kind="ExternalInput").ap()
        self.g_wT = nc.dram_tensor("g_wT", [D, E], F32, kind="ExternalInput").ap()
        self.e_w = nc.dram_tensor("e_w", [E, D, D], F32, kind="ExternalInput").ap()
        self.e_b = nc.dram_tensor("e_b", [E, D], F32, kind="ExternalInput").ap()
        self.y = nc.dram_tensor("y", [S2, D], F32, kind="ExternalOutput").ap()
        self.a_hbm = nc.dram_tensor("a_scr", [D], F32).ap()
        self.gt_hbm = nc.dram_tensor("gt_scr", [NTO * E, 128], BF16).ap()

        es0 = contextlib.ExitStack()
        e = es0.enter_context
        self.ident_bf = e(nc.sbuf_tensor("ident_bf", [128, 128], BF16))
        self.iota_f = e(nc.sbuf_tensor("iota_f", [128, 128], F32))
        self.iota_p = e(nc.sbuf_tensor("iota_p", [128, 1], F32))
        self.ones_c = e(nc.sbuf_tensor("ones_c", [128, 1], BF16))
        self.gw_sb = e(nc.sbuf_tensor("gw_sb", [128, ND, E], BF16))
        self.eb_sb = e(nc.sbuf_tensor("eb_sb", [E, D], BF16))
        self.st1 = e(nc.sbuf_tensor("st1", [128, NT, 32], F32))
        self.st2 = e(nc.sbuf_tensor("st2", [128, NTO, 32], F32))
        self.kst = e(nc.sbuf_tensor("kst", [128, NT, 4], F32))
        self.rn_bf = e(nc.sbuf_tensor("rn_bf", [128, 8], BF16))
        self.aT = e(nc.sbuf_tensor("aT", [128, ND], F32))
        self.gates = e(nc.sbuf_tensor("gates", [128, NTO, E], F32))
        self.logit = e(nc.sbuf_tensor("logit", [128, NTO, E], F32))
        self.gst = e(nc.sbuf_tensor("gst", [128, NTO, 8], F32))
        self.eq1s = e(nc.sbuf_tensor("eq1s", [128, NTO, E], F32))
        self.l2s = e(nc.sbuf_tensor("l2s", [128, NTO, E], F32))
        self.gb_scr = e(nc.sbuf_tensor("gb_scr", [128, NTO, E], BF16))
        self.gTb = e(nc.sbuf_tensor("gTb", [8, NTO, 128], BF16))
        self.x_scr = [e(nc.sbuf_tensor(f"x_scr{i}", [128, D], F32)) for i in range(2)]
        self.tmp8 = e(nc.sbuf_tensor("tmp8", [128, 4, RH], BF16))
        self.ct_scr = [e(nc.sbuf_tensor(f"ct{i}", [128, RH], BF16))
                       for i in range(2)]
        self.sn_scr = [e(nc.sbuf_tensor(f"sn{i}", [128, RH], BF16))
                       for i in range(2)]
        self.bb_scr = e(nc.sbuf_tensor("bb_scr", [128, D], BF16, side="right"))
        self.xln_scr = [e(nc.sbuf_tensor(f"xln_scr{i}", [128, D], BF16)) for i in range(2)]
        self.wbuf = [e(nc.sbuf_tensor(f"wbuf{i}", [128, ND, OCW], BF16))
                     for i in range(2)]
        self.sxp = [Sem(e(nc.semaphore(f"sxp{i}"))) for i in range(2)]
        self.sct = [Sem(e(nc.semaphore(f"sct{i}"))) for i in range(2)]
        self.swp = [Sem(e(nc.semaphore(f"swp{i}"))) for i in range(2)]
        self.sbb = Sem(e(nc.semaphore("sbb")))
        self.s0w = Sem(e(nc.semaphore("s0w")))
        self.sa = Sem(e(nc.semaphore("sa")))
        self.sgt = Sem(e(nc.semaphore("sgt")))
        self.sow = Sem(e(nc.semaphore("sow")))
        self.sg = Sem(e(nc.semaphore("sg")))
        self.sc = Sem(e(nc.semaphore("sc")))
        self.sv = Sem(e(nc.semaphore("sv")))
        self.sp = Sem(e(nc.semaphore("sp")))
        self.so = Sem(e(nc.semaphore("so")))

        self.phase0()

        esA = contextlib.ExitStack()
        self.xlnT = esA.enter_context(nc.sbuf_tensor("xlnT", [128, ND, S2], BF16))
        esKV = contextlib.ExitStack()
        self.k_sb = esKV.enter_context(nc.sbuf_tensor("k_sb", [128, NTO, D], BF16))
        self.v_sb = esKV.enter_context(nc.sbuf_tensor("v_sb", [128, NTO, D], BF16))
        with contextlib.ExitStack() as ps_es, nc.Block() as blk:
            self._mk_psum(ps_es)
            ps_a = [ps_es.enter_context(
                nc.psum_tensor(self._nm("ps_a") + str(i), [1, OCW], F32))
                for i in range(NQ)]
            for sp_i in (1, 0):
                self._proj_bias(blk, "k")
                w_done_k0 = self._get_wdma(blk, "k", 0)

                def k0_hook(t, blk=blk, w_done_k0=w_done_k0):
                    if t % 2 == 0:
                        self._proj_flush()
                    self._proj_qt(blk, "k", 0, t, w_done_k0,
                                  first_t=(t == 0), defer_evac=True)
                self.phase_ln1(blk, sp_i, after_tile=k0_hook)
                self._proj_flush()
                self.free["wbuf0"] = (self.sp, self.sp.n)
                self.phase_proj(blk, sp_i, "k", start_q=1)
                tpq = NTO // NQ

                def after_q(q, sp_i=sp_i, blk=blk, tpq=tpq):
                    if q == NQ - 1:
                        return
                    for t in range(q * tpq, (q + 1) * tpq):
                        self._rope_norm_k_tile(blk, sp_i, t)
                self.phase_proj(blk, sp_i, "v", after_quarter=after_q)
                self.phase_pa(blk, sp_i, ps_a, late=range((NQ - 1) * tpq, NTO))
            self._emit_wdma(blk, "q", 0)
            self.phase_a_roundtrip(blk)
        esKV.close()

        esQ = contextlib.ExitStack()
        esX2 = contextlib.ExitStack()
        self.x2T = esX2.enter_context(
            nc.sbuf_tensor("x2T", [128, ND, S2], BF16, side="right"))
        self.q_sb = esQ.enter_context(
            nc.sbuf_tensor("q_sb", [128, NTO, D], BF16, side="right"))
        esQT = contextlib.ExitStack()
        with contextlib.ExitStack() as ps_es, nc.Block() as blk:
            self._mk_psum(ps_es)
            @blk.gpsimd
            def _(g):
                g.dma_start(
                    out=self.x2T[:, :, 0:OCW],
                    in_=self.w_oT.rearrange("(dc p) o -> p dc o", p=128)
                    [:, :, 0:OCW]).then_inc(self.sow.h, 16)
                self.sow.n += 16
            self.phase_proj(blk, 0, "q")
            esA.close()
            self.qT = esQT.enter_context(
                nc.sbuf_tensor("qT", [128, ND, S2], BF16))
            self.phase_rope_q_transpose(blk)
            for t in range(2):
                xs = self.x_scr[t]
                sxp = self.sxp[t]

                @blk.sync
                def _(s, t=t, xs=xs, sxp=sxp):
                    self._wait_free(s, f"x_scr{t}")
                    if t == 0:
                        s.wait_ge(self.sa.h, self.a_store_done)
                    s.dma_start(out=xs[:, :],
                                in_=self.x_all[t * 128:(t + 1) * 128, :]
                                ).then_inc(sxp.h, 16)
                    sxp.n += 16
                self.x1init_dma = getattr(self, "x1init_dma", {})
                self.x1init_dma[t] = sxp.n
        esQ.close()
        esX1 = contextlib.ExitStack()
        self.x1 = esX1.enter_context(
            nc.sbuf_tensor("x1", [128, NTO, D], F32, side="right"))
        with contextlib.ExitStack() as ps_es, nc.Block() as blk:
            self._mk_psum(ps_es, n_m=4)
            ps_l = ps_es.enter_context(
                nc.psum_tensor(self._nm("ps_l"), [128, E], F32))
            ps_gt = ps_es.enter_context(
                nc.psum_tensor(self._nm("ps_gt"), [self.NTO * 8, 128], BF16))
            self._proj_bias(blk, "o")
            self.phase_x1init(blk, tiles=[0, 1])
            self._emit_wdma(blk, "o", 1)
            self.phase_oproj(blk, ps_l)
            self.phase_gates(blk, ps_gt)
            self.phase_experts(blk)
            self.phase_biasmix(blk)
        self.phase_store()
        esX1.close()
        esQT.close()
        esX2.close()
        es0.close()
        return self.nc

    def _mk_psum(self, ps_es, tr=True, n_m=2):
        nc, OCW = self.nc, self.OCW
        if tr:
            self.ps_tr = [ps_es.enter_context(
                nc.psum_tensor(self._nm("ps_tr") + str(i), [128, 1024], BF16))
                for i in range(2)]
            for i in range(2):
                self.free.pop(f"tr{i}", None)
        self.ps_m = [ps_es.enter_context(
            nc.psum_tensor(self._nm("ps_m") + str(i), [128, OCW], F32))
            for i in range(n_m)]
        for i in range(4):
            self.free.pop(f"psm{i}", None)

    def phase0(self):
        nc = self.nc
        s0w, sg = self.s0w, self.sg
        with nc.Block() as blk:
            @blk.gpsimd
            def _(g):
                g.dma_start(out=self.gw_sb[:, :, :],
                            in_=self.g_wT.rearrange("(dc p) e -> p dc e", p=128)
                            ).then_inc(s0w.h, 16)
                s0w.n += 16
                g.dma_start(out=self.eb_sb[:, :], in_=self.e_b[:, :]
                            ).then_inc(s0w.h, 16)
                s0w.n += 16
                g.iota(self.iota_f[:, :], [[1, 128]], channel_multiplier=0,
                       allow_small_or_imprecise_dtypes=True)
                g.iota(self.iota_p[:, :1], [[1, 1]], channel_multiplier=1,
                       allow_small_or_imprecise_dtypes=True)
                sg.inc(g.memset(self.ones_c[:, :], 1.0))

            @blk.vector
            def _(v):
                v.wait_ge(sg.h, sg.n)
                v.tensor_scalar(out=self.ident_bf[:, :], in0=self.iota_f[:, :],
                                scalar1=self.iota_p[:, 0:1], scalar2=None,
                                op0=ALU.is_equal)

    def _ln_stats_tile(self, blk, stats, x_in):
        D = self.D
        sbuf, si = stats
        nch = D // 512 if D >= 512 else 1
        cw = D // nch

        @blk.vector
        def _(v):
            v.drain()
            for c in range(nch):
                v.bn_stats(out=sbuf[:, si, 6 * c:6 * c + 6],
                           in_=x_in[:, c * cw:(c + 1) * cw])
            v.drain()
            v.bn_aggr(out=sbuf[:, si, 24:26], in_=sbuf[:, si, 0:6 * nch])

    def _ln_chain(self, blk, sbuf, lo, n):
        sc, sv = self.sc, self.sv
        mean = sbuf[:, lo:lo + n, 24:25]
        var = sbuf[:, lo:lo + n, 25:26]
        vpe = sbuf[:, lo:lo + n, 26:27]
        recip = sbuf[:, lo:lo + n, 27:28]
        rstd = sbuf[:, lo:lo + n, 28:29]
        negmr = sbuf[:, lo:lo + n, 29:30]
        opr = sbuf[:, lo:lo + n, 30:31]

        @blk.vector
        def _(v):
            v.drain()
            v.tensor_scalar(out=vpe, in0=var, scalar1=LN_EPS, scalar2=None,
                            op0=ALU.add)
            v.drain()
            sv.inc(v.reciprocal(out=recip, in_=vpe))
        recip_done = sv.n

        @blk.scalar
        def _(s):
            s.wait_ge(sv.h, recip_done)
            sc.inc(s.activation(out=rstd, in_=recip, func=AF.Sqrt))
        sqrt_done = sc.n

        @blk.vector
        def _(v):
            v.wait_ge(sc.h, sqrt_done)
            v.scalar_tensor_tensor(out=negmr, in0=mean, scalar=-1.0, in1=rstd,
                                   op0=ALU.mult, op1=ALU.mult)
            sv.inc(v.tensor_scalar(out=opr, in0=rstd, scalar1=1.0,
                                   scalar2=None, op0=ALU.add))
        return sv.n

    def _ln_apply(self, blk, stats, x_in, out_bf, ready, xkey):
        sv = self.sv
        sbuf, si = stats

        @blk.vector
        def _(v):
            v.wait_ge(sv.h, ready)
            self._wait_free(v, xkey)
            v.drain()
            sv.inc(v.tensor_scalar(out=out_bf, in0=x_in,
                                   scalar1=sbuf[:, si, 28:29],
                                   scalar2=sbuf[:, si, 29:30],
                                   op0=ALU.mult, op1=ALU.add))
        return sv.n

    def _transpose_tile(self, blk, src_of, dstT, tblock, scalar_evac=True,
                        scale_of=None):
        ND = self.ND
        sp, sv = self.sp, self.sv
        nb = (ND + 7) // 8
        for b in range(nb):
            ps = self.ps_tr[b % 2]
            lo = b * 8
            hi = min(lo + 8, ND)

            @blk.tensor
            def _(t, ps=ps, lo=lo, hi=hi, b=b):
                self._wait_free(t, f"tr{b % 2}")
                for dc in range(lo, hi):
                    tr = t.transpose(ps[:, (dc - lo) * 128:(dc - lo + 1) * 128],
                                     src_of(dc), self.ident_bf[:, :])
                sp.inc(tr)
            tr_done = self.sp.n
            if scale_of is not None:
                if b % 2 == 0:
                    @blk.vector
                    def _(v, ps=ps, lo=lo, hi=hi, tr_done=tr_done, b=b):
                        v.wait_ge(sp.h, tr_done)
                        if tblock == 0 and b == 0:
                            v.wait_ge(self.sa.h, self.aT_load_done)
                        for dc in range(lo, hi):
                            ins = v.tensor_scalar(
                                out=dstT[:, dc,
                                         tblock * 128:(tblock + 1) * 128],
                                in0=ps[:, (dc - lo) * 128:(dc - lo + 1) * 128],
                                scalar1=scale_of(dc), scalar2=None,
                                op0=ALU.mult)
                        sv.inc(ins)
                    self.free[f"tr{b % 2}"] = (sv, sv.n)
                else:
                    @blk.scalar
                    def _(s, ps=ps, lo=lo, hi=hi, tr_done=tr_done):
                        s.wait_ge(sp.h, tr_done)
                        if tblock == 0:
                            s.wait_ge(self.sa.h, self.aT_load_done)
                        for dc in range(lo, hi):
                            ins = s.activation(
                                out=dstT[:, dc,
                                         tblock * 128:(tblock + 1) * 128],
                                in_=ps[:, (dc - lo) * 128:(dc - lo + 1) * 128],
                                func=AF.Identity, scale=scale_of(dc))
                        self.sc.inc(ins)
                    self.free[f"tr{b % 2}"] = (self.sc, self.sc.n)
            elif b % 2 == 0 or not scalar_evac:
                @blk.vector
                def _(v, ps=ps, lo=lo, hi=hi, tr_done=tr_done):
                    v.wait_ge(sp.h, tr_done)
                    sv.inc(v.tensor_copy(
                        out=dstT[:, lo:hi, tblock * 128:(tblock + 1) * 128],
                        in_=ps[:, 0:(hi - lo) * 128]))
                self.free[f"tr{b % 2}"] = (sv, sv.n)
            else:
                @blk.scalar
                def _(s, ps=ps, lo=lo, hi=hi, tr_done=tr_done):
                    s.wait_ge(sp.h, tr_done)
                    self.sc.inc(s.activation(
                        out=dstT[:, lo:hi, tblock * 128:(tblock + 1) * 128],
                        in_=ps[:, 0:(hi - lo) * 128], func=AF.Copy))
                self.free[f"tr{b % 2}"] = (self.sc, self.sc.n)
        self.free["xln_scr_last_tr"] = (sp, self.sp.n)
        self.tile_ready[tblock] = (self.sv.n, self.sc.n)

    def phase_ln1(self, blk, sp_i, after_tile=None):
        NTO = self.NTO
        sc, sv = self.sc, self.sv
        self.tile_ready = {}
        BL = 2
        for t0 in range(0, NTO, BL):
            for t in range(t0, t0 + BL):
                g = sp_i * NTO + t
                xs = self.x_scr[t % 2]
                sxp = self.sxp[t % 2]

                @blk.sync
                def _(s, t=t, g=g, xs=xs, sxp=sxp):
                    self._wait_free(s, f"x_scr{t % 2}")
                    s.dma_start(out=xs[:, :],
                                in_=self.x_all[g * 128:(g + 1) * 128, :]
                                ).then_inc(sxp.h, 16)
                    sxp.n += 16
                dma_done = sxp.n

                @blk.vector
                def _(v, dma_done=dma_done, sxp=sxp):
                    v.wait_ge(sxp.h, dma_done)
                self._ln_stats_tile(blk, (self.st1, g), xs[:, :])
            ready = self._ln_chain(blk, self.st1, sp_i * NTO + t0, BL)
            for t in range(t0, t0 + BL):
                g = sp_i * NTO + t
                xs = self.x_scr[t % 2]
                xln = self.xln_scr[t % 2]
                apply_done = self._ln_apply(blk, (self.st1, g), xs[:, :],
                                            xln[:, :], ready, f"xlnf{t % 2}")
                self.free[f"x_scr{t % 2}"] = (sv, apply_done)

                @blk.tensor
                def _(te, apply_done=apply_done):
                    te.wait_ge(sv.h, apply_done)
                self._transpose_tile(
                    blk, lambda dc, xln=xln: xln[:, dc * 128:(dc + 1) * 128],
                    self.xlnT, t)
                self.free[f"xlnf{t % 2}"] = (self.sp, self.sp.n)
            if after_tile is not None:
                for t in range(t0, t0 + BL):
                    after_tile(t)

    def _emit_wdma(self, blk, which, q):
        OCW = self.OCW
        wT = {"q": self.w_qT, "k": self.w_kT, "v": self.w_vT,
              "o": self.w_oT}[which]
        swp = self.swp[q % 2]

        @blk.gpsimd
        def _(g, q=q, swp=swp, wT=wT):
            self._wait_free(g, f"wbuf{q % 2}")
            g.dma_start(
                out=self.wbuf[q % 2][:, :, :],
                in_=wT.rearrange("(dc p) o -> p dc o", p=128)
                [:, :, q * OCW:(q + 1) * OCW]).then_inc(swp.h, 16)
            swp.n += 16
        self.wdma_done[(which, q)] = swp.n

    def _get_wdma(self, blk, which, q):
        if (which, q) not in self.wdma_done:
            self._emit_wdma(blk, which, q)
        return self.wdma_done.pop((which, q))

    def _proj_bias(self, blk, which):
        brow = {"q": 0, "k": 1, "v": 2, "o": 3}[which]
        sbb = self.sbb

        @blk.gpsimd
        def _(g, sbb=sbb, brow=brow):
            self._wait_free(g, "bb_scr")
            g.dma_start(out=self.bb_scr[:, :],
                        in_=self.biases[brow:brow + 1, :]
                        .partition_broadcast(128)).then_inc(sbb.h, 16)
            sbb.n += 16
        self.bias_done = sbb.n

    def _proj_qt(self, blk, which, q, t, w_done, first_t, defer_evac=False):
        NTO, OCW, ND = self.NTO, self.OCW, self.ND
        sc, sv, sp = self.sc, self.sv, self.sp
        dst = {"k": getattr(self, "k_sb", None),
               "v": getattr(self, "v_sb", None),
               "q": getattr(self, "q_sb", None)}[which]
        bias_done = self.bias_done
        self.psm_ctr += 1
        nm_ = len(self.ps_m)
        ps = self.ps_m[self.psm_ctr % nm_]
        pnm = f"psm{self.psm_ctr % nm_}"
        tile_rdy = self.tile_ready.get(t)

        @blk.tensor
        def _(te, q=q, t=t, ps=ps, w_done=w_done, pnm=pnm):
            te.wait_ge(self.swp[q % 2].h, w_done)
            if q == 0 and tile_rdy is not None:
                te.wait_ge(sv.h, tile_rdy[0])
                te.wait_ge(self.sc.h, tile_rdy[1])
            self._wait_free(te, pnm)
            for dc in range(ND):
                mm = te.matmul(
                    ps[:, :],
                    self.xlnT[:, dc, t * 128:(t + 1) * 128],
                    self.wbuf[q % 2][:, dc, :],
                    start=(dc == 0), stop=(dc == ND - 1))
            sp.inc(mm)
        mmc = sp.n
        if defer_evac:
            self._pq_pend.append((blk, which, q, t, ps, pnm, mmc))
        else:
            self._proj_evac(blk, which, q, t, ps, pnm, mmc)

    def _proj_evac(self, blk, which, q, t, ps, pnm, mmc):
        OCW = self.OCW
        sv, sp = self.sv, self.sp
        dst = {"k": getattr(self, "k_sb", None),
               "v": getattr(self, "v_sb", None),
               "q": getattr(self, "q_sb", None)}[which]
        bias_done = self.bias_done

        @blk.vector
        def _(v, q=q, t=t, ps=ps, mmc=mmc, bias_done=bias_done):
            v.wait_ge(sp.h, mmc)
            v.wait_ge(self.sbb.h, bias_done)
            sv.inc(v.tensor_add(
                out=dst[:, t, q * OCW:(q + 1) * OCW],
                in0=ps[:, :],
                in1=self.bb_scr[:, q * OCW:(q + 1) * OCW]))
        self.free[pnm] = (sv, sv.n)
        self.kv_evac[(which, t)] = sv.n

    def _proj_flush(self):
        pend, self._pq_pend = self._pq_pend, []
        for (blk, which, q, t, ps, pnm, mmc) in pend:
            self._proj_evac(blk, which, q, t, ps, pnm, mmc)

    def phase_proj(self, blk, sp_i, which, after_quarter=None, start_q=0):
        NTO, NQ = self.NTO, self.NQ
        sp = self.sp
        if start_q == 0:
            self._proj_bias(blk, which)
        for q in range(start_q, NQ):
            w_done = self._get_wdma(blk, which, q)
            for t in range(NTO):
                self._proj_qt(blk, which, q, t, w_done, first_t=(t == 0))
            self.free[f"wbuf{q % 2}"] = (sp, sp.n)
            if after_quarter is not None:
                after_quarter(q)
        self.free["bb_scr"] = (self.sv, self.sv.n)

    def _rope_tile(self, blk, buf, t, g, which):
        RH = self.RH
        sv = self.sv
        sct = self.sct[t % 2]
        ct = self.ct_scr[t % 2]
        sn = self.sn_scr[t % 2]

        @blk.sync
        def _(s, t=t, g=g, sct=sct):
            self._wait_free(s, f"ct{t % 2}")
            s.dma_start(out=ct[:, :],
                        in_=self.cos_t[g * 128:(g + 1) * 128, :]
                        ).then_inc(sct.h, 16)
            s.dma_start(out=sn[:, :],
                        in_=self.sin_t[g * 128:(g + 1) * 128, :]
                        ).then_inc(sct.h, 16)
            sct.n += 32
        tab_done = sct.n

        @blk.vector
        def _(v, t=t, tab_done=tab_done, sct=sct):
            v.wait_ge(sct.h, tab_done)
            self._wait_free(v, "tmp8")
            v.drain()
            ev = buf[:, t, 0:RH]
            od = buf[:, t, RH:2 * RH]
            rt = [self.tmp8[:, i, :] for i in range(4)]
            v.tensor_mul(out=rt[0], in0=ev, in1=ct[:, :])
            v.tensor_mul(out=rt[1], in0=od, in1=sn[:, :])
            v.tensor_mul(out=rt[2], in0=od, in1=ct[:, :])
            v.tensor_mul(out=rt[3], in0=ev, in1=sn[:, :])
            v.drain()
            v.tensor_sub(out=ev, in0=rt[0], in1=rt[1])
            sv.inc(v.tensor_add(out=od, in0=rt[2], in1=rt[3]))
        self.free[f"ct{t % 2}"] = (sv, sv.n)
        self.free["tmp8"] = (sv, sv.n)

    def _rope_norm_k_tile(self, blk, sp_i, t):
        sv = self.sv
        g = sp_i * self.NTO + t
        self._rope_tile(blk, self.k_sb, t, g, "k")

        @blk.vector
        def _(v, t=t, g=g):
            self._wait_free(v, "tmp8")
            v.drain()
            sv.inc(v.scalar_tensor_tensor(
                out=self.tmp8.rearrange("p a b -> p (a b)"),
                in0=self.k_sb[:, t, :], scalar=1.0,
                in1=self.k_sb[:, t, :], op0=ALU.mult, op1=ALU.mult,
                accum_out=self.kst[:, g, 0:1]))
        self.free["tmp8"] = (sv, sv.n)
        self.ssq_done = getattr(self, "ssq_done", {})
        self.ssq_done[g] = sv.n

    def _rnorm_batch(self, blk, g0, lo, hi):
        sv = self.sv
        ssq = self.kst[:, g0 + lo:g0 + hi, 0:1]
        nrm = self.kst[:, g0 + lo:g0 + hi, 1:2]
        clm = self.kst[:, g0 + lo:g0 + hi, 2:3]
        rnm = self.kst[:, g0 + lo:g0 + hi, 3:4]
        ssq_all = self.ssq_done[g0 + hi - 1]

        @blk.scalar
        def _(s, ssq_all=ssq_all, ssq=ssq, nrm=nrm):
            s.wait_ge(sv.h, ssq_all)
            self.sc.inc(s.activation(out=nrm, in_=ssq, func=AF.Sqrt))
        nrm_done = self.sc.n

        @blk.vector
        def _(v, nrm_done=nrm_done, nrm=nrm, clm=clm, rnm=rnm, lo=lo, hi=hi):
            v.wait_ge(self.sc.h, nrm_done)
            v.tensor_scalar(out=clm, in0=nrm, scalar1=NORM_EPS, scalar2=None,
                            op0=ALU.max)
            v.drain()
            v.reciprocal(out=rnm, in_=clm)
            v.drain()
            sv.inc(v.tensor_copy(out=self.rn_bf[:, lo:hi], in_=rnm))
        return sv.n

    def phase_pa(self, blk, sp_i, ps_a, late=()):
        NTO, NQ, OCW = self.NTO, self.NQ, self.OCW
        sv, sp = self.sv, self.sp
        first = sp_i == 1
        last = sp_i == 0
        late = set(late)
        order = [t for t in range(NTO) if t not in late] + sorted(late)
        g0 = sp_i * NTO
        n_early = NTO - len(late)
        rn_early = self._rnorm_batch(blk, g0, 0, n_early) if n_early else None
        rn_of = lambda t: rn_early if t < n_early else rn_late

        def pa_tile(t):
            is_late = t in late
            @blk.vector
            def _(v, t=t, is_late=is_late):
                if t == order[0] or is_late:
                    v.drain()
                sv.inc(v.tensor_mul(out=self.v_sb[:, t, :],
                                    in0=self.k_sb[:, t, :],
                                    in1=self.v_sb[:, t, :]))
            p_done = sv.n

            is_first = first and t == order[0]
            is_last = last and t == order[-1]

            @blk.tensor
            def _(te, t=t, p_done=p_done, is_first=is_first, is_last=is_last,
                  rn_done=rn_of(t)):
                te.wait_ge(sv.h, p_done)
                te.wait_ge(sv.h, rn_done)
                for qi in range(NQ):
                    mm = te.matmul(ps_a[qi][:, :], self.rn_bf[:, t:t + 1],
                                   self.v_sb[:, t, qi * OCW:(qi + 1) * OCW],
                                   start=is_first, stop=is_last)
                if is_last:
                    sp.inc(mm)

        for t in order[:n_early]:
            pa_tile(t)
        for t in sorted(late):
            self._rope_norm_k_tile(blk, sp_i, t)
        rn_late = self._rnorm_batch(blk, g0, n_early, NTO) if late else None
        for t in order[n_early:]:
            pa_tile(t)
        if last:
            a_done = self.sp.n

            @blk.vector
            def _(v):
                v.wait_ge(sp.h, a_done)
                self._wait_free(v, "x_scr0")
                v.drain()
                for qi in range(NQ):
                    ins = v.tensor_copy(
                        out=self.x_scr[0][0:1, qi * OCW:(qi + 1) * OCW],
                        in_=ps_a[qi][:, :])
                self.sv.inc(ins)
            self.free["x_scr0"] = (self.sv, self.sv.n)
            self.a_row_ready = self.sv.n

    def phase_a_roundtrip(self, blk):
        sa, sv = self.sa, self.sv
        a_done = self.a_row_ready

        @blk.sync
        def _(s):
            s.wait_ge(sv.h, a_done)
            s.dma_start(out=self.a_hbm[None, :], in_=self.x_scr[0][0:1, :]
                        ).then_inc(sa.h, 16)
            sa.n += 16
            s.wait_ge(sa.h, sa.n)
            self.a_store_done = sa.n
            s.dma_start(out=self.aT[:, :],
                        in_=self.a_hbm.rearrange("(dc p) -> p dc", p=128),
                        allow_slow_non_contiguous=True,
                        ).then_inc(sa.h, 16)
            sa.n += 16
        self.aT_load_done = sa.n

    def phase_rope_q_transpose(self, blk):
        ND, NTO = self.ND, self.NTO
        sa, sv = self.sa, self.sv
        self.tile_ready = {}
        for t in range(NTO):
            self._rope_tile(blk, self.q_sb, t, t, "q")
            rope_done = sv.n

            @blk.tensor
            def _(te, rope_done=rope_done):
                te.wait_ge(sv.h, rope_done)
            self._transpose_tile(
                blk, lambda dc, t=t: self.q_sb[:, t, dc * 128:(dc + 1) * 128],
                self.qT, t,
                scale_of=lambda dc: self.aT[:, dc:dc + 1])

    def _scale_wquarter(self, blk, which, q):
        ND = self.ND
        sv, sa = self.sv, self.sa
        swp = self.swp[q % 2]
        w_done = self._get_wdma(blk, which, q)

        @blk.vector
        def _(v, q=q, w_done=w_done, swp=swp):
            v.wait_ge(swp.h, w_done)
            v.wait_ge(sa.h, self.aT_load_done)
            for dc in range(ND):
                ins = v.tensor_scalar(out=self.wbuf[q % 2][:, dc, :],
                                      in0=self.wbuf[q % 2][:, dc, :],
                                      scalar1=self.aT[:, dc:dc + 1],
                                      scalar2=None, op0=ALU.mult)
            sv.inc(ins)
        self.wscale_done = getattr(self, "wscale_done", {})
        self.wscale_done[q] = (sv.n, w_done)

    def phase_x1init(self, blk, tiles=None):
        NTO = self.NTO
        sv = self.sv
        if tiles is None:
            self._proj_bias(blk, "o")
            tiles = range(NTO)
        for t in tiles:
            xs = self.x_scr[t % 2]
            sxp = self.sxp[t % 2]
            if t < 2:
                dma_done = self.x1init_dma[t]
            else:
                @blk.sync
                def _(s, t=t, xs=xs, sxp=sxp):
                    self._wait_free(s, f"x_scr{t % 2}")
                    s.dma_start(out=xs[:, :],
                                in_=self.x_all[t * 128:(t + 1) * 128, :]
                                ).then_inc(sxp.h, 16)
                    sxp.n += 16
                dma_done = sxp.n

            @blk.vector
            def _(v, t=t, xs=xs, dma_done=dma_done, sxp=sxp):
                v.wait_ge(sxp.h, dma_done)
                if t == 0:
                    v.wait_ge(self.sbb.h, self.bias_done)
                v.drain()
                v.tensor_scalar(out=self.x1[:, t, :], in0=xs[:, :],
                                scalar1=self.st1[:, t, 30:31],
                                scalar2=self.st1[:, t, 29:30],
                                op0=ALU.mult, op1=ALU.add)
                v.drain()
                sv.inc(v.tensor_add(out=self.x1[:, t, :],
                                    in0=self.x1[:, t, :],
                                    in1=self.bb_scr[:, :]))
            self.free[f"x_scr{t % 2}"] = (sv, sv.n)

    def phase_oproj(self, blk, ps_l):
        NTO, NQ, OCW, ND = self.NTO, self.NQ, self.OCW, self.ND
        sv, sp = self.sv, self.sp
        self.tile_ready = {}
        for q in range(NQ):
            if q == 0:
                swp, w_done = self.sow, 16
                rhs_of = lambda dc: self.x2T[:, dc, 0:OCW]
            else:
                swp = self.swp[q % 2]
                w_done = self._get_wdma(blk, "o", q)
                rhs_of = lambda dc, q=q: self.wbuf[q % 2][:, dc, :]
            for t in range(NTO):
                nm_ = len(self.ps_m)
                ps = self.ps_m[(q * NTO + t) % nm_]
                pnm = f"psm{(q * NTO + t) % nm_}"

                @blk.tensor
                def _(te, q=q, t=t, ps=ps, w_done=w_done, swp=swp, pnm=pnm,
                      rhs_of=rhs_of):
                    te.wait_ge(swp.h, w_done)
                    self._wait_free(te, pnm)
                    for dc in range(ND):
                        mm = te.matmul(
                            ps[:, :],
                            self.qT[:, dc, t * 128:(t + 1) * 128],
                            rhs_of(dc),
                            start=(dc == 0), stop=(dc == ND - 1))
                    sp.inc(mm)
                mmc = sp.n

                @blk.vector
                def _(v, q=q, t=t, ps=ps, mmc=mmc):
                    v.wait_ge(sp.h, mmc)
                    if q == 0:
                        v.drain()
                    sv.inc(v.scalar_tensor_tensor(
                        out=self.x1[:, t, q * OCW:(q + 1) * OCW],
                        in0=ps[:, :], scalar=1.0,
                        in1=self.x1[:, t, q * OCW:(q + 1) * OCW],
                        op0=ALU.mult, op1=ALU.add))
                self.free[pnm] = (sv, sv.n)
                if q == 0 and t + 2 < NTO:
                    self.phase_x1init(blk, tiles=[t + 2])
                if q == NQ - 1:
                    self._ln2_tile(blk, ps_l, t)
            if q != 0:
                self.free[f"wbuf{q % 2}"] = (sp, sp.n)
        self.free["bb_scr"] = (sv, self.sv.n)

    def _ln2_tile(self, blk, ps_l, t):
        self._ln_stats_tile(blk, (self.st2, t), self.x1[:, t, :])
        self._ln2_pend.append(t)
        if len(self._ln2_pend) == 4 or t == self.NTO - 1:
            tiles = self._ln2_pend
            self._ln2_pend = []
            ready = self._ln_chain(blk, self.st2, tiles[0], len(tiles))
            for tt in tiles:
                self._ln2_tail(blk, ps_l, tt, ready)

    def _ln2_tail(self, blk, ps_l, t, ready):
        ND = self.ND
        sc, sv, sp, s0w = self.sc, self.sv, self.sp, self.s0w
        xln = self.xln_scr[t % 2]
        apply_done = self._ln_apply(blk, (self.st2, t), self.x1[:, t, :],
                                    xln[:, :], ready, f"xlnf{t % 2}")

        @blk.tensor
        def _(te, apply_done=apply_done):
            te.wait_ge(sv.h, apply_done)
        self._transpose_tile(
            blk, lambda dc, xln=xln: xln[:, dc * 128:(dc + 1) * 128],
            self.x2T, t, scalar_evac=False)
        self.free[f"xlnf{t % 2}"] = (self.sp, self.sp.n)
        x2T_done = self.tile_ready[t]

        @blk.tensor
        def _(te, t=t, x2T_done=x2T_done):
            te.wait_ge(s0w.h, 32)
            self._wait_free(te, "ps_l")
            te.wait_ge(sv.h, x2T_done[0])
            te.wait_ge(sc.h, x2T_done[1])
            for dc in range(ND):
                mm = te.matmul(ps_l[:, :],
                               self.x2T[:, dc, t * 128:(t + 1) * 128],
                               self.gw_sb[:, dc, :],
                               start=(dc == 0), stop=(dc == ND - 1))
            self.sp.inc(mm)
        lg_done = self.sp.n

        @blk.vector
        def _(v, t=t, lg_done=lg_done):
            v.wait_ge(sp.h, lg_done)
            sv.inc(v.tensor_copy(out=self.logit[:, t, :], in_=ps_l[:, :]))
        self.free["ps_l"] = (sv, sv.n)

    def phase_gates(self, blk, ps_gt):
        NTO, E = self.NTO, self.E
        sc, sv, sp = self.sc, self.sv, self.sp
        L = self.logit[:, :, :]
        m1 = self.gst[:, :, 0:1]
        m2 = self.gst[:, :, 1:2]
        w2 = self.gst[:, :, 2:3]
        w1 = self.gst[:, :, 3:4]

        def bc(ap):
            return ap.broadcast_to([128, NTO, E])

        @blk.vector
        def _(v):
            v.drain()
            v.tensor_reduce(out=m1, in_=L, axis=AXX, op=ALU.max)
            v.drain()
            v.tensor_tensor(out=self.eq1s[:, :, :], in0=L, in1=bc(m1),
                            op=ALU.is_equal)
            v.drain()
            v.scalar_tensor_tensor(out=self.l2s[:, :, :],
                                   in0=self.eq1s[:, :, :],
                                   scalar=-BIG_NEG, in1=L,
                                   op0=ALU.mult, op1=ALU.add)
            v.drain()
            v.tensor_reduce(out=m2, in_=self.l2s[:, :, :], axis=AXX,
                            op=ALU.max)
            v.drain()
            sv.inc(v.tensor_tensor(out=w2, in0=m2, in1=m1, op=ALU.subtract))
        d_done = sv.n

        @blk.scalar
        def _(s, d_done=d_done):
            s.wait_ge(sv.h, d_done)
            sc.inc(s.activation(out=w2, in_=w2, func=AF.Sigmoid))
        sig_done = sc.n

        @blk.vector
        def _(v, sig_done=sig_done):
            v.wait_ge(sc.h, sig_done)
            v.tensor_scalar(out=w1, in0=w2, scalar1=-1.0, scalar2=1.0,
                            op0=ALU.mult, op1=ALU.add)
            v.drain()
            v.tensor_tensor(out=self.l2s[:, :, :], in0=self.l2s[:, :, :],
                            in1=bc(m2), op=ALU.is_equal)
            v.drain()
            v.tensor_tensor(out=self.l2s[:, :, :], in0=self.l2s[:, :, :],
                            in1=bc(w2), op=ALU.mult)
            v.tensor_tensor(out=self.eq1s[:, :, :], in0=self.eq1s[:, :, :],
                            in1=bc(w1), op=ALU.mult)
            v.drain()
            v.tensor_tensor(out=self.gates[:, :, :], in0=self.eq1s[:, :, :],
                            in1=self.l2s[:, :, :], op=ALU.add)
            v.drain()
            sv.inc(v.tensor_copy(out=self.gb_scr[:, :, :],
                                 in_=self.gates[:, :, :]))
        g_done = sv.n

        @blk.tensor
        def _(te, g_done=g_done):
            te.wait_ge(sv.h, g_done)
            self.sp.inc(te.transpose(
                ps_gt[:, :],
                self.gb_scr.rearrange("p t e -> p (t e)"),
                self.ident_bf[:, :]))
        gt_done = self.sp.n

        gpk = self.tmp8.rearrange("p a b -> p (a b)")[0:NTO * E, 0:128]

        @blk.vector
        def _(v, gt_done=gt_done):
            v.wait_ge(sp.h, gt_done)
            self._wait_free(v, "tmp8")
            sv.inc(v.tensor_copy(out=gpk, in_=ps_gt[:, :]))
        self.free["ps_gt_ev"] = (sv, sv.n)
        pk_done = sv.n

        @blk.sync
        def _(s, pk_done=pk_done, gpk=gpk):
            s.wait_ge(sv.h, pk_done)
            s.dma_start(out=self.gt_hbm[:, :], in_=gpk
                        ).then_inc(self.sgt.h, 16)
            self.sgt.n += 16
            s.wait_ge(self.sgt.h, self.sgt.n)
            s.dma_start(out=self.gTb[:, :, :],
                        in_=self.gt_hbm.rearrange("(t e) n -> e t n", e=8)
                        ).then_inc(self.sgt.h, 16)
            self.sgt.n += 16
        self.free["ps_gt"] = (self.sgt, self.sgt.n)

    def phase_biasmix(self, blk):
        NTO, NQ, OCW = self.NTO, self.NQ, self.OCW
        sv, sp, s0w = self.sv, self.sp, self.s0w
        so = self.so
        gTb_sem, gTb_done = self.free["ps_gt"]
        for t in range(NTO):
            for q in range(NQ):
                nm_ = len(self.ps_m)
                ps = self.ps_m[(t * NQ + q) % nm_]
                pnm = f"psm{(t * NQ + q) % nm_}"

                @blk.tensor
                def _(te, t=t, q=q, ps=ps, pnm=pnm):
                    te.wait_ge(s0w.h, 32)
                    if (t, q) == (0, 0):
                        te.wait_ge(gTb_sem.h, gTb_done)
                    self._wait_free(te, pnm)
                    self.sp.inc(te.matmul(
                        ps[:, :], self.gTb[:, t, :],
                        self.eb_sb[:, q * OCW:(q + 1) * OCW],
                        start=True, stop=True))
                mm_done = self.sp.n

                @blk.vector
                def _(v, t=t, q=q, ps=ps, mm_done=mm_done):
                    v.wait_ge(sp.h, mm_done)
                    if (t, q) == (0, 0):
                        v.drain()
                    sv.inc(v.scalar_tensor_tensor(
                        out=self.x1[:, t, q * OCW:(q + 1) * OCW],
                        in0=ps[:, :], scalar=1.0,
                        in1=self.x1[:, t, q * OCW:(q + 1) * OCW],
                        op0=ALU.mult, op1=ALU.add))
                self.free[pnm] = (sv, sv.n)
            tile_done = sv.n

            @blk.sync
            def _(s, t=t, tile_done=tile_done):
                s.wait_ge(sv.h, tile_done)
                s.dma_start(out=self.y[t * 128:(t + 1) * 128, :],
                            in_=self.x1[:, t, :]).then_inc(so.h, 16)
                so.n += 16

    def phase_experts(self, blk):
        ND, NTO, NQ, OCW, E = self.ND, self.NTO, self.NQ, self.OCW, self.E
        sv, sp = self.sv, self.sp
        steps = [(e_i, q) for e_i in range(E) for q in range(NQ)]
        for si, (e_i, q) in enumerate(steps):
            swp = self.swp[si % 2]

            @blk.gpsimd
            def _(g, si=si, e_i=e_i, q=q, swp=swp):
                self._wait_free(g, f"wbuf{si % 2}")
                g.dma_start(
                    out=self.wbuf[si % 2][:, :, :],
                    in_=self.e_w.rearrange("e (dc p) o -> e p dc o", p=128)
                    [e_i, :, :, q * OCW:(q + 1) * OCW]).then_inc(swp.h, 16)
                swp.n += 16
            w_done = swp.n
            for t in range(NTO):
                nm_ = len(self.ps_m)
                ps = self.ps_m[(si * NTO + t) % nm_]
                pnm = f"psm{(si * NTO + t) % nm_}"

                x2rdy = self.tile_ready[t]

                @blk.tensor
                def _(te, si=si, t=t, ps=ps, w_done=w_done, swp=swp, pnm=pnm,
                      x2rdy=x2rdy):
                    te.wait_ge(swp.h, w_done)
                    if si == 0:
                        te.wait_ge(sv.h, x2rdy[0])
                        te.wait_ge(self.sc.h, x2rdy[1])
                    self._wait_free(te, pnm)
                    for dc in range(ND):
                        mm = te.matmul(
                            ps[:, :],
                            self.x2T[:, dc, t * 128:(t + 1) * 128],
                            self.wbuf[si % 2][:, dc, :],
                            start=(dc == 0), stop=(dc == ND - 1))
                    self.sp.inc(mm)
                mmc = self.sp.n

                @blk.vector
                def _(v, e_i=e_i, q=q, t=t, ps=ps, mmc=mmc):
                    v.wait_ge(sp.h, mmc)
                    if t == 0:
                        v.drain()
                    sv.inc(v.scalar_tensor_tensor(
                        out=self.x1[:, t, q * OCW:(q + 1) * OCW],
                        in0=ps[:, :], scalar=self.gates[:, t, e_i:e_i + 1],
                        in1=self.x1[:, t, q * OCW:(q + 1) * OCW],
                        op0=ALU.mult, op1=ALU.add))
                self.free[pnm] = (sv, sv.n)
            self.free[f"wbuf{si % 2}"] = (sp, self.sp.n)

    def phase_store(self):
        so = self.so
        with self.nc.Block() as blk:
            @blk.sync
            def _(s):
                s.wait_ge(so.h, so.n)


def _host_prep(L0_attn_W_in, L0_attn_b_in, L0_attn_W_out, L0_attn_b_out,
               L0_ffn_gate_w, L0_ffn_expert_w, L0_ffn_expert_b, S, D):
    rot = D // 2
    perm = np.concatenate([np.arange(0, rot, 2), np.arange(1, rot, 2),
                           np.arange(rot, D)])
    w_q = L0_attn_W_in[0:D][perm]
    w_k = L0_attn_W_in[D:2 * D][perm]
    w_v = L0_attn_W_in[2 * D:3 * D][perm]
    w_qT = np.ascontiguousarray(w_q.T)
    w_kT = np.ascontiguousarray(w_k.T)
    w_vT = np.ascontiguousarray(w_v.T)
    w_oT = np.ascontiguousarray(L0_attn_W_out[:, perm].T)
    biases = np.ascontiguousarray(np.stack([
        L0_attn_b_in[0:D][perm], L0_attn_b_in[D:2 * D][perm],
        L0_attn_b_in[2 * D:3 * D][perm], L0_attn_b_out]).astype(np.float32))
    g_wT = np.ascontiguousarray(L0_ffn_gate_w.T)
    e_w = np.ascontiguousarray(L0_ffn_expert_w)
    e_b = np.ascontiguousarray(L0_ffn_expert_b)
    inv = 1.0 / (10000.0 ** (np.arange(0, rot, 2, dtype=np.float64) / rot))
    ang = np.arange(S, dtype=np.float64)[:, None] * inv[None, :]
    cos_t = np.cos(ang).astype(ml_dtypes.bfloat16)
    sin_t = np.sin(ang).astype(ml_dtypes.bfloat16)
    return dict(w_qT=w_qT, w_kT=w_kT, w_vT=w_vT, w_oT=w_oT, biases=biases,
                g_wT=g_wT, e_w=e_w, e_b=e_b, cos_t=cos_t, sin_t=sin_t)


def _in_maps(X, shared, S, n_cores):
    S2 = S // 2
    maps = []
    for c in range(n_cores):
        b, h = c // 2, c % 2
        xb = np.ascontiguousarray(X[b])
        if h == 0:
            x_all, cos_c, sin_c = xb, shared["cos_t"], shared["sin_t"]
        else:
            x_all = np.ascontiguousarray(
                np.concatenate([xb[S2:], xb[:S2]], axis=0))
            cos_c = np.ascontiguousarray(np.concatenate(
                [shared["cos_t"][S2:], shared["cos_t"][:S2]], axis=0))
            sin_c = np.ascontiguousarray(np.concatenate(
                [shared["sin_t"][S2:], shared["sin_t"][:S2]], axis=0))
        m = dict(shared)
        m["x_all"] = x_all
        m["cos_t"] = cos_c
        m["sin_t"] = sin_c
        maps.append(m)
    return maps


def kernel(L0_attn_X, L0_attn_W_in, L0_attn_b_in, L0_attn_W_out, L0_attn_b_out,
           L0_ffn_gate_w, L0_ffn_expert_w, L0_ffn_expert_b):
    B, S, D = L0_attn_X.shape
    E = L0_ffn_gate_w.shape[0]
    shared = _host_prep(np.asarray(L0_attn_W_in, dtype=np.float32),
                        np.asarray(L0_attn_b_in, dtype=np.float32),
                        np.asarray(L0_attn_W_out, dtype=np.float32),
                        np.asarray(L0_attn_b_out, dtype=np.float32),
                        np.asarray(L0_ffn_gate_w, dtype=np.float32),
                        np.asarray(L0_ffn_expert_w, dtype=np.float32),
                        np.asarray(L0_ffn_expert_b, dtype=np.float32), S, D)
    X = np.asarray(L0_attn_X, dtype=np.float32)
    maps = _in_maps(X, shared, S, N_CORES)
    nc = Builder(S, D, E, 512).build()
    res = run_bass_kernel_spmd(nc, maps, core_ids=list(range(N_CORES)))
    S2 = S // 2
    out = np.empty((B, S, D), dtype=np.float32)
    for c in range(N_CORES):
        b, h = c // 2, c % 2
        out[b, h * S2:(h + 1) * S2] = res.results[c]["y"]
    return out
